# version 11
# baseline (speedup 1.0000x reference)
import numpy as np
from contextlib import ExitStack

import concourse.bass as bass
import concourse.mybir as mybir
from concourse.bass_utils import run_bass_kernel_spmd

F32 = mybir.dt.float32
BF16 = mybir.dt.bfloat16
AF = mybir.ActivationFunctionType
ALU = mybir.AluOpType
AX = mybir.AxisListType

S = 2048
DM = 1024
NT = S // 128
NC = DM // 128
NCORES = 8
NFP = 2176
NFT = NFP // 128

ENGS = ("tensor", "scalar", "vector", "gpsimd", "sync")



class B:

    def __init__(self):
        self.ops = {e: [] for e in ENGS}
        self.cnt = {}

    def tok(self, name):
        return (name, self.cnt.get(name, 0))

    def emit(self, eng, fn, waits=(), inc=None, inc_amt=1):
        w = tuple((t[0], t[1]) for t in waits if t is not None and t[1] > 0)
        incs = []
        if inc is not None:
            for nm, amt in ([(inc, inc_amt)] if isinstance(inc, str) else inc):
                self.cnt[nm] = self.cnt.get(nm, 0) + amt
                incs.append((nm, amt))
        self.ops[eng].append((fn, w, tuple(incs)))
        return self.tok(incs[0][0]) if incs else None

    def dma(self, dst, src, waits=(), eng="gpsimd", sem="dg"):
        return self.emit(eng, lambda e, d=dst, s=src: e.dma_start(d, s),
                         waits=waits, inc=sem, inc_amt=16)

    def wait_only(self, eng, waits):
        self.ops[eng].append((lambda e: e.nop(), tuple(
            (t[0], t[1]) for t in waits if t is not None and t[1] > 0), ()))

    def join(self, waiters, producers=None):
        names = list(self.cnt) if producers is None else list(producers)
        toks = [self.tok(p) for p in names if self.cnt.get(p, 0) > 0]
        for eng in waiters:
            self.wait_only(eng, toks)


def _run_engine(nc, b, sems, name, eng):
    with nc.allow_low_precision(reason="bf16 matmul operands by design"):
        for fn, waits, incs in b.ops[name]:
            for (sn, v) in waits:
                eng.wait_ge(sems[sn], v)
            inst = fn(eng)
            for (sn, amt) in incs:
                inst.then_inc(sems[sn], amt)


def build_nc(build_fn):
    nc = bass.Bass("TRN2", target_bir_lowering=False, debug=False)
    b = B()
    with ExitStack() as stack:
        build_fn(nc, stack, b)
        sems = {}
        for name in b.cnt:
            sems[name] = stack.enter_context(nc.semaphore(name))
        block = stack.enter_context(nc.Block())
        for engname in ENGS:
            if b.ops[engname]:
                getattr(block, engname)(
                    lambda eng, en=engname: _run_engine(nc, b, sems, en, eng))
    return nc



def _dft_consts():
    N, L = 2 * S, S
    NF = N // 2 + 1
    t = np.arange(L, dtype=np.float64)
    f = np.arange(NFP, dtype=np.float64)
    ang = 2.0 * np.pi * np.outer(t, f) / N
    Wc = np.cos(ang)
    Ws = -np.sin(ang)
    Wc[:, NF:] = 0.0
    Ws[:, NF:] = 0.0
    alpha = np.full(NFP, 2.0)
    alpha[0] = 1.0
    alpha[NF - 1] = 1.0
    alpha[NF:] = 0.0
    ang2 = 2.0 * np.pi * np.outer(f, t) / N
    Cr = alpha[:, None] * np.cos(ang2) / N
    Ci = -alpha[:, None] * np.sin(ang2) / N
    Wf = np.stack([Wc, Ws]).reshape(2, 16, 128, NFT, 128)
    Wf = np.ascontiguousarray(Wf.transpose(0, 3, 1, 2, 4)).astype(np.float32)
    Cv = np.stack([Cr, Ci]).reshape(2, NFT, 128, 16, 128)
    Cv = np.ascontiguousarray(Cv.transpose(0, 1, 3, 2, 4)).astype(np.float32)
    return Wf, Cv


def _consts():
    c = {}
    c["ident"] = np.eye(128, dtype=np.float32)
    fc = np.zeros((128, 8), dtype=np.float32)
    fc[:, 0] = 1e-5
    fc[:, 1] = 1.0
    c["fconst"] = fc
    c["ones"] = np.ones((128, 128), dtype=np.float32)
    m = np.zeros((128, 4, 512), dtype=np.float32)
    for j in range(4):
        for p in range(128):
            m[p, j, p + j * 128:] = 1.0
    c["cmask"] = np.ascontiguousarray(m.reshape(128, 2048))
    c["tvals"] = np.broadcast_to(np.arange(S, dtype=np.float32), (128, S)).copy()
    bd = np.zeros((128, 16, 128), dtype=np.float32)
    for j in range(16):
        for p in range(128):
            bd[p, j, 8 * j + p // 16] = 1.0
    c["bdiag"] = np.ascontiguousarray(bd.reshape(128, 16 * 128))
    import ml_dtypes
    wf, cv = _dft_consts()
    c["dftW"] = np.ascontiguousarray(wf.transpose(1, 0, 2, 3, 4)).astype(
        ml_dtypes.bfloat16)
    c["dftC"] = np.ascontiguousarray(cv.transpose(1, 0, 2, 3, 4)).astype(
        ml_dtypes.bfloat16)
    return c


def _dn_layout(a):
    out = np.empty((128, 128), dtype=np.float32)
    d = np.arange(128) // 16
    n = np.arange(128) % 16
    for dt in range(128):
        out[:, dt] = a[dt * 8 + d, n]
    return np.ascontiguousarray(out)


def prep_inputs(inputs):
    f32 = lambda a: np.ascontiguousarray(np.asarray(a, dtype=np.float32))
    P = {}
    P["x"] = f32(inputs["L0_attn_x"])
    P["l0_qT"] = f32(np.asarray(inputs["L0_attn_q_proj"]).T)
    P["l0_kT"] = f32(np.asarray(inputs["L0_attn_k_proj"]).T)
    P["l0_vT"] = f32(np.asarray(inputs["L0_attn_v_proj"]).T)
    P["l0_oT"] = f32(np.asarray(inputs["L0_attn_o_proj"]).T)
    P["l0_routerT"] = f32(np.asarray(inputs["L0_ffn_router_w"]).T)
    P["l0_w1t"] = f32(np.asarray(inputs["L0_ffn_w1"]).transpose(0, 2, 1))
    P["l0_w2t"] = f32(np.asarray(inputs["L0_ffn_w2"]).transpose(0, 2, 1))
    b1 = np.asarray(inputs["L0_ffn_b1"])
    P["l0_b1"] = f32(b1.reshape(8, 32, 128).transpose(2, 0, 1).reshape(128, 256))
    P["l0_b2"] = f32(np.asarray(inputs["L0_ffn_b2"]).reshape(1, 8192))
    P["l1_delta"] = _dn_layout(np.asarray(inputs["L1_attn_delta"])[:, :, 0])
    P["l1_alpha"] = _dn_layout(np.asarray(inputs["L1_attn_alpha"])[:, :, 0])
    P["l1_beta"] = _dn_layout(np.asarray(inputs["L1_attn_beta"])[:, :, 0])
    P["l1_gamma"] = _dn_layout(np.asarray(inputs["L1_attn_gamma"]))
    P["l1_omega_b"] = f32(np.broadcast_to(np.asarray(inputs["L1_attn_omega"]), (128, 1024)))
    P["l1_ln1_b"] = f32(np.broadcast_to(np.asarray(inputs["L1_ffn_ln1_w"]), (128, 1024)))
    P["l1_ln2_b"] = f32(np.broadcast_to(np.asarray(inputs["L1_ffn_ln2_w"]), (128, 1024)))
    P["l1_cattn"] = f32(inputs["L1_ffn_c_attn_w"])
    P["l1_cproj"] = f32(inputs["L1_ffn_c_proj_w"])
    P["l1_router"] = f32(inputs["L1_ffn_router_w"])
    P["l1_w1"] = f32(inputs["L1_ffn_w1"])
    P["l1_w2"] = f32(inputs["L1_ffn_w2"])
    P.update(_consts())
    return P


def cmaj(dram_ap, r0, r1, c0, c1):
    return dram_ap[r0:r1, c0:c1].rearrange("(t p) c -> p t c", p=128)


def sb3(sbuf_ap, nt):
    return sbuf_ap.rearrange("p (t c) -> p t c", t=nt)



def emit_ln(b, x_ap, sq, st, make_dst, eps_ap, norm_free=None, wrow=None,
            x_ready=None):
    inv = 1.0 / DM
    toks = []
    t_red_prev = None
    for ti in range(NT):
        o = (ti % 2) * 8
        xt = x_ap[:, ti * DM:(ti + 1) * DM]
        stat_free = [toks[ti - 2]] if ti >= 2 else []
        if x_ready:
            stat_free = stat_free + list(x_ready(ti))
        b.emit("vector", lambda e, xt=xt, o=o: e.reduce_sum(
            st[:, o + 0:o + 1], xt, axis=AX.X, negate=True),
            waits=stat_free, inc="dve")
        sq_waits = [t_red_prev] if t_red_prev else []
        if x_ready:
            sq_waits = sq_waits + list(x_ready(ti))
        t_sq = b.emit("scalar", lambda e, xt=xt: e.activation(
            sq[:], xt, AF.Square), waits=sq_waits, inc="act")
        t_red_prev = b.emit("vector", lambda e, o=o: e.reduce_sum(
            st[:, o + 1:o + 2], sq[:], axis=AX.X), waits=[t_sq], inc="dve")
        t_mean = b.emit("vector", lambda e, o=o: e.tensor_scalar_mul(
            st[:, o + 2:o + 3], st[:, o + 0:o + 1], inv),
            waits=[b.tok("dve")], inc="dve")
        t_m2 = b.emit("vector", lambda e, o=o: e.tensor_mul(
            st[:, o + 3:o + 4], st[:, o + 2:o + 3], st[:, o + 2:o + 3]),
            waits=[t_mean], inc="dve")
        t_var = b.emit("vector", lambda e, o=o: e.scalar_tensor_tensor(
            st[:, o + 4:o + 5], st[:, o + 1:o + 2], inv, st[:, o + 3:o + 4],
            ALU.mult, ALU.subtract), waits=[t_m2], inc="dve")
        t_std = b.emit("scalar", lambda e, o=o: e.activation(
            st[:, o + 5:o + 6], st[:, o + 4:o + 5], AF.Sqrt, bias=eps_ap),
            waits=[t_var], inc="act")
        t_rstd = b.emit("vector", lambda e, o=o: e.reciprocal(
            st[:, o + 6:o + 7], st[:, o + 5:o + 6]), waits=[t_std], inc="dve")
        t_nmr = b.emit("vector", lambda e, o=o: e.tensor_mul(
            st[:, o + 7:o + 8], st[:, o + 2:o + 3], st[:, o + 6:o + 7]),
            waits=[t_rstd], inc="dve")
        waits = [t_nmr] + (norm_free(ti) if norm_free else [])
        if wrow is None:
            t_norm = b.emit("scalar", lambda e, ti=ti, o=o, d=make_dst(ti):
                            e.activation(
                                d, x_ap[:, ti * DM:(ti + 1) * DM], AF.Identity,
                                scale=st[:, o + 6:o + 7], bias=st[:, o + 7:o + 8]),
                            waits=waits, inc="act")
        else:
            t_na = b.emit("scalar", lambda e, ti=ti, o=o: e.activation(
                sq[:], x_ap[:, ti * DM:(ti + 1) * DM], AF.Identity,
                scale=st[:, o + 6:o + 7], bias=st[:, o + 7:o + 8]),
                waits=waits + [t_red_prev], inc="act")
            t_norm = b.emit("vector", lambda e, ti=ti, d=make_dst(ti):
                            e.tensor_mul(d, sq[:], wrow),
                            waits=[t_na], inc="dve")
            t_red_prev = t_norm
        toks.append(t_norm)
    return toks


def emit_lnt_to_lnc(b, lnt, lnc, pt, ident, norm_toks, extra_wait=()):
    idx = 0
    ev = {}
    for ti in range(NT):
        for ci in range(NC):
            off = (idx % 2) * 1024 + ((idx // 2) % 4) * 128
            waits = [norm_toks[ti]] + list(extra_wait)
            if idx >= 2:
                waits.append(ev[idx - 2])
            if idx >= 8:
                waits.append(ev[idx - 8])
            t_tr = b.emit("tensor", lambda e, ti=ti, ci=ci, off=off: e.transpose(
                pt[:, off:off + 128],
                lnt[:, ti * DM + ci * 128:ti * DM + ci * 128 + 128], ident),
                waits=waits, inc="pe")
            ev[idx] = b.emit("scalar", lambda e, ti=ti, ci=ci, off=off: e.copy(
                lnc[:, ci * S + ti * 128:ci * S + ti * 128 + 128],
                pt[:, off:off + 128]),
                waits=[t_tr], inc="act")
            idx += 1
    ttok = [ev[(ti + 1) * NC - 1] for ti in range(NT)]
    return ev[idx - 1], b.tok("pe"), ttok



def emit_attention(b, u, lnc, dst, w_srcs, wo_src, lnc_tok, aoT_free,
                   dst_waits, wbuf_free, scratch_free, lnc_tt=None):
    qT, kT, vON, probsT, aoT = u["qT"], u["kT"], u["vON"], u["probsT"], u["aoT"]
    wqkv, wo = u["wqkv"], u["wo"]
    ones, cmask = u["ones"], u["cmask"]
    invr, tmp64 = u["invr"], u["tmp64"]
    ps, psy, ptf = u["ps"], u["psy"], u["ptf"]
    W = 1024
    NG = 8
    if lnc_tt is None:
        lnc_tt = [lnc_tok] * NT

    grp_tok = {}

    def issue_group_dma(g, waits):
        par = g % 2
        wb = par * 3 * W
        sem = "wg%d" % par
        wq_ap, wk_ap, wv_ap = w_srcs(g)
        b.dma(sb3(wqkv[:, wb:wb + W], NC), wq_ap, waits=waits, sem=sem)
        b.dma(sb3(wqkv[:, wb + W:wb + 2 * W], NC), wk_ap, sem=sem)
        grp_tok[g] = b.dma(sb3(wqkv[:, wb + 2 * W:wb + 3 * W], NC), wv_ap, sem=sem)

    issue_group_dma(0, list(wbuf_free))
    issue_group_dma(1, [])

    bank_ev = {0: None, 1: None, 2: None, 3: None}
    av_free = {0: None, 1: None}
    bc_ev = {0: None, 1: None}
    tmp_ev = {0: None, 1: None}
    last_n = None
    first_n = True
    grp_free = {}
    for g in range(NG):
        par = g % 2
        wb = par * 3 * W
        wtok = grp_tok[g]
        qkev = {}
        for wi, woff in ((0, 0), (1, W)):
            dstbuf = qT if wi == 0 else kT
            for tc in range(4):
                i = wi * 4 + tc
                bank = ps[i % 2]
                for ct in range(NC):
                    w0 = wb + woff + ct * 128
                    waits = ()
                    if ct == 0:
                        waits = [wtok, lnc_tt[tc * 4 + 3], bank_ev[i % 2]]
                        if g == 0 and i < 2:
                            waits = waits + list(scratch_free)
                    b.emit("tensor", lambda e, bank=bank, ct=ct, tc=tc, w0=w0:
                           e.matmul(bank[:, :], wqkv[:, w0:w0 + 128],
                                    lnc[:, ct * S + tc * 512:ct * S + tc * 512 + 512],
                                    start=(ct == 0), stop=(ct == NC - 1)),
                           waits=waits, inc=("pe" if ct == NC - 1 else None))
                t_mm = b.tok("pe")
                if i % 2 == 1:
                    t_ev = b.emit("scalar", lambda e, dstbuf=dstbuf, tc=tc:
                                  e.copy(dstbuf[:, (tc - 1) * 512:(tc + 1) * 512],
                                         u["psA"][:, :]),
                                  waits=[t_mm], inc="act")
                    bank_ev[0] = t_ev
                    bank_ev[1] = t_ev
                    qkev[i] = t_ev
        t_qk = qkev[7]
        t_z = b.emit("vector", lambda e: e.memset(vON[:, :], 0.0),
                     waits=([last_n] if last_n else list(scratch_free)),
                     inc="dve")
        t_vset = b.emit("vector", lambda e: e.memset(
            vON[:, 64:NT * 192:192], 1.0), waits=[t_z], inc="dve")
        t_vev = {}
        for tt in range(NT):
            bank = ps[2 + tt % 2]
            for ct in range(NC):
                waits = ()
                if ct == 0:
                    waits = [wtok, lnc_tt[tt], bank_ev[2 + tt % 2],
                             t_vev.get(tt - 2)]
                b.emit("tensor", lambda e, bank=bank, ct=ct, tt=tt, wb=wb:
                       e.matmul(bank[:, 0:128],
                                lnc[:, ct * S + tt * 128:ct * S + tt * 128 + 128],
                                wqkv[:, wb + 2 * W + ct * 128:wb + 2 * W + ct * 128 + 128],
                                start=(ct == 0), stop=(ct == NC - 1)),
                       waits=waits, inc=("pe" if ct == NC - 1 else None))
            t_mm = b.tok("pe")
            grp_free[g] = t_mm
            for hh in range(2):
                t_vev[tt] = b.emit("vector", lambda e, tt=tt, hh=hh, bank=bank:
                                   e.tensor_copy(
                                       vON[:, tt * 192 + hh * 128:
                                           tt * 192 + hh * 128 + 64],
                                       bank[:, hh * 64:hh * 64 + 64]),
                                   waits=([t_mm, t_vset] if hh == 0 else ()),
                                   inc="dve")
        t_v = t_vev[NT - 1]
        bank_ev[2] = t_v
        bank_ev[3] = t_v
        for qc in range(4):
            nkt = 4 * qc + 4
            av_bank = {0: psy[:, 0:512], 1: psy[:, 512:1024]}
            psum_pair_ap = {0: u["psA"][:, :], 1: u["psB"][:, :]}
            pslot = lambda hh, kt: (hh * 4 + kt % 4) * 512
            t_exq = {}
            av_pe = {}

            def emit_score(hh, kt, qc=qc):
                pbase = 64 * hh
                sbank = ps[2 * hh + kt % 2]
                t_sc = b.emit("tensor", lambda e, sbank=sbank, kt=kt, qc=qc,
                              pbase=pbase: e.matmul(
                                  sbank[:, :],
                                  kT[pbase:pbase + 64, kt * 128:kt * 128 + 128],
                                  qT[pbase:pbase + 64, qc * 512:qc * 512 + 512],
                                  start=True, stop=True),
                              waits=[t_qk, bank_ev[2 * hh + kt % 2]],
                              inc="pe")
                po = pslot(hh, kt)
                t_ex = b.emit("scalar", lambda e, sbank=sbank, po=po: e.activation(
                    probsT[:, po:po + 512], sbank[:, :], AF.Exp, scale=0.125),
                    waits=[t_sc, av_pe.get((hh, kt - 4))], inc="act")
                bank_ev[2 * hh + kt % 2] = t_ex
                jj = kt - 4 * qc
                if jj >= 0:
                    t_ex = b.emit("vector", lambda e, po=po, jj=jj: e.tensor_mul(
                        probsT[:, po:po + 512], probsT[:, po:po + 512],
                        cmask[:, jj * 512:jj * 512 + 512]),
                        waits=[t_ex], inc="dve")
                t_exq[(hh, kt)] = t_ex

            def emit_av(hh, kt, qc=qc, nkt=nkt):
                po = pslot(hh, kt)
                waits_av = [t_exq[(hh, kt)]]
                if kt == 0:
                    waits_av += [av_free[hh], t_v]
                if hh == 0:
                    b.emit("tensor", lambda e, kt=kt, nkt=nkt, po=po: e.matmul(
                        av_bank[0][0:65, :],
                        vON[:, kt * 192:kt * 192 + 65],
                        probsT[:, po:po + 512],
                        start=(kt == 0), stop=(kt == nkt - 1)),
                        waits=waits_av, inc="pe")
                else:
                    b.emit("tensor", lambda e, kt=kt, nkt=nkt, po=po: e.matmul(
                        av_bank[1][0:128, :],
                        vON[:, kt * 192 + 64:kt * 192 + 192],
                        probsT[:, po:po + 512],
                        start=(kt == 0), stop=(kt == nkt - 1)),
                        waits=waits_av, inc="pe")
                av_pe[(hh, kt)] = b.tok("pe")

            for kt in range(nkt):
                emit_score(0, kt)
                emit_score(1, kt)
                if kt > 1:
                    emit_av(0, kt - 2)
                    emit_av(1, kt - 2)
            for k2 in (nkt - 2, nkt - 1):
                emit_av(0, k2)
                emit_av(1, k2)
            for hh in range(2):
                pbase = 64 * hh
                srow = 64 if hh == 0 else 0
                bcb = ptf[:, hh * 512:hh * 512 + 512]
                t_av = av_pe[(hh, nkt - 1)]
                t_inv = b.emit("vector", lambda e, hh=hh, srow=srow: e.reciprocal(
                    invr[srow:srow + 1, :], av_bank[hh][srow:srow + 1, :]),
                    waits=[t_av, bc_ev[hh]], inc="dve")
                t_bc = b.emit("tensor", lambda e, srow=srow, bcb=bcb: e.matmul(
                    bcb, ones[srow:srow + 1, 0:128],
                    invr[srow:srow + 1, :], start=True, stop=True),
                    waits=[t_inv, bc_ev[hh] if bc_ev[hh] else lnc_tok], inc="pe")
                t_tmp = b.emit("vector", lambda e, hh=hh, pbase=pbase:
                               e.tensor_copy(
                                   tmp64[pbase:pbase + 64, :],
                                   av_bank[hh][pbase:pbase + 64, :]),
                               waits=[t_av, tmp_ev[hh]], inc="dve")
                waits_n = [t_tmp, t_bc]
                if first_n:
                    waits_n += list(aoT_free)
                t_n = b.emit("vector", lambda e, hh=hh, g=g, qc=qc,
                             pbase=pbase, bcb=bcb: e.tensor_mul(
                                 aoT[pbase:pbase + 64,
                                     g * S + qc * 512:g * S + qc * 512 + 512],
                                 tmp64[pbase:pbase + 64, :],
                                 bcb[pbase:pbase + 64, :]),
                             waits=waits_n, inc="dve")
                first_n = False
                last_n = t_n
                bc_ev[hh] = t_n
                av_free[hh] = t_n
                tmp_ev[hh] = t_n
        if g + 2 < NG:
            issue_group_dma(g + 2, [grp_free[g]])
    t_ao = b.tok("dve")
    t_heads_pe = b.tok("pe")
    oev = {0: None, 1: None}
    for dh in range(2):
        t_wo = b.dma(sb3(wo[:, :], 8), wo_src(dh), sem="wo",
                     waits=(list(wbuf_free) if dh == 0 else [b.tok("pe")]))
        for tt in range(NT):
            bank = tt % 2
            for ft in range(8):
                waits = ()
                if ft == 0:
                    waits = [t_ao, t_wo, oev[bank], bank_ev[bank]]
                b.emit("tensor", lambda e, tt=tt, bank=bank, ft=ft: e.matmul(
                    ps[bank][:, :],
                    aoT[:, ft * S + tt * 128:ft * S + tt * 128 + 128],
                    wo[:, ft * 512:ft * 512 + 512],
                    start=(ft == 0), stop=(ft == 7)),
                    waits=waits, inc=("pe" if ft == 7 else None))
            t_mm = b.tok("pe")
            oev[bank] = b.emit("vector", lambda e, tt=tt, dh=dh, bank=bank: e.tensor_add(
                dst[:, tt * 1024 + dh * 512:tt * 1024 + dh * 512 + 512],
                dst[:, tt * 1024 + dh * 512:tt * 1024 + dh * 512 + 512],
                ps[bank][:, :]),
                waits=[t_mm] + list(dst_waits(tt)), inc="dve")
    return b.tok("dve"), t_heads_pe



def emit_moe(b, u, lnc, dst, E, F, fchunk, w1_src, w2_src, router_src,
             b1_col, b2_src, lnc_tok, dst_waits, scratch_free, wbuf_free,
             lnc_tt=None, wbuf2_free=None):
    ps, psy = u["ps"], u["psy"]
    h, w1b, w2b = u["h"], u["w1b"], u["w2b"]
    logits, mvec, sc8 = u["logits"], u["mvec"], u["sc8"]
    router_sb, ones = u["router_sb"], u["ones"]
    nq = F // fchunk
    nfs = fchunk // 128
    if lnc_tt is None:
        lnc_tt = [lnc_tok] * NT
    w1w = NC * fchunk
    w2w = nfs * 1024

    t_rt = b.dma(sb3(router_sb[:, :NC * E], NC), router_src, sem="dmr",
                 waits=list(scratch_free))
    rt_ev = {}
    for ts in range(NT):
        bank = ps[2 + ts % 2]
        for ct in range(NC):
            waits = ()
            if ct == 0:
                waits = [t_rt, lnc_tt[ts], rt_ev.get(ts - 2)] + (
                    list(scratch_free) if ts < 2 else [])
            b.emit("tensor", lambda e, bank=bank, ct=ct, ts=ts: e.matmul(
                bank[:, 0:E],
                lnc[:, ct * S + ts * 128:ct * S + ts * 128 + 128],
                router_sb[:, ct * E:ct * E + E],
                start=(ct == 0), stop=(ct == NC - 1)),
                waits=waits, inc=("pe" if ct == NC - 1 else None))
        t_mm = b.tok("pe")
        rt_ev[ts] = b.emit("scalar", lambda e, ts=ts, bank=bank: e.copy(
            logits[:, ts * E:ts * E + E], bank[:, 0:E]),
            waits=[t_mm], inc="act")
    mv_toks = []
    for ts in range(NT):
        lg = logits[:, ts * E:ts * E + E]
        t0 = b.emit("vector", lambda e, lg=lg: e.reduce_max(
            sc8[:, 8:9], lg, axis=AX.X), waits=[rt_ev[ts], b.tok("dve")],
            inc="dve")
        t1 = b.emit("vector", lambda e, lg=lg: e.tensor_scalar(
            sc8[:, 0:8], lg, sc8[:, 8:9], None, ALU.is_equal),
            waits=[t0], inc="dve")
        t2 = b.emit("vector", lambda e, lg=lg: e.scalar_tensor_tensor(
            sc8[:, 16:24], sc8[:, 0:8], -1e30, lg, ALU.mult, ALU.add),
            waits=[t1], inc="dve")
        t3 = b.emit("vector", lambda e: e.reduce_max(
            sc8[:, 9:10], sc8[:, 16:24], axis=AX.X), waits=[t2], inc="dve")
        t4 = b.emit("vector", lambda e: e.tensor_sub(
            sc8[:, 10:11], sc8[:, 9:10], sc8[:, 8:9]), waits=[t3], inc="dve")
        t5 = b.emit("scalar", lambda e: e.activation(
            sc8[:, 11:12], sc8[:, 10:11], AF.Sigmoid), waits=[t4], inc="act")
        t6 = b.emit("vector", lambda e: e.tensor_scalar(
            sc8[:, 12:13], sc8[:, 11:12], -1.0, 1.0, ALU.mult, ALU.add),
            waits=[t5], inc="dve")
        t7 = b.emit("vector", lambda e, lg=lg: e.tensor_scalar(
            sc8[:, 16:24], lg, sc8[:, 9:10], None, ALU.is_equal),
            waits=[t6], inc="dve")
        t8 = b.emit("vector", lambda e: e.tensor_scalar_mul(
            sc8[:, 16:24], sc8[:, 16:24], sc8[:, 11:12]), waits=[t7], inc="dve")
        t9 = b.emit("vector", lambda e, ts=ts: e.scalar_tensor_tensor(
            mvec[:, ts * E:ts * E + E], sc8[:, 0:8], sc8[:, 12:13],
            sc8[:, 16:24], ALU.mult, ALU.add), waits=[t8], inc="dve")
        mv_toks.append(t9)
    t_mv = mv_toks[-1]

    mvT_tok = {}
    t_b2all = None
    if b2_src is not None:
        mvT = u["lnt"][0:8, 6400:6400 + NT * 128]
        mvc = u["lnt"][:, 6144:6144 + NT * 8]
        t_b2all = b.dma(u["b2r"][0:8, :], b2_src, waits=list(scratch_free),
                        sem="db2")
        tr_ev = {}
        for ts in range(NT):
            slot = ts % 2
            t_cst = b.emit("vector", lambda e, ts=ts: e.tensor_copy(
                mvc[:, ts * 8:ts * 8 + 8], mvec[:, ts * 8:ts * 8 + 8]),
                waits=[mv_toks[ts]], inc="dve")
            t_tr = b.emit("tensor", lambda e, ts=ts, slot=slot: e.transpose(
                u["pt"][0:8, slot * 1024:slot * 1024 + 128],
                mvc[:, ts * 8:ts * 8 + 8], u["ident"][:, :]),
                waits=[t_cst, tr_ev.get(ts - 2)] + (list(scratch_free)
                                                    if ts < 2 else []),
                inc="pe")
            tr_ev[ts] = b.emit("scalar", lambda e, ts=ts, slot=slot: e.copy(
                mvT[0:8, ts * 128:ts * 128 + 128],
                u["pt"][0:8, slot * 1024:slot * 1024 + 128]),
                waits=[t_tr], inc="act")
            mvT_tok[ts] = tr_ev[ts]

    nchunks = E * nq
    w_tok = {}

    def issue_w1_dma(i, waits):
        par = i % 2
        e_, q_ = i // nq, i % nq
        b.dma(sb3(w1b[:, par * w1w:par * w1w + w1w], NC), w1_src(e_, q_),
              waits=waits, sem="wf%d" % par)

    def issue_w2_dma(i, waits):
        par = i % 2
        e_, q_ = i // nq, i % nq
        w_tok[i] = b.dma(sb3(w2b[:, par * w2w:par * w2w + w2w], nfs),
                         w2_src(e_, q_), waits=waits, sem="wf%d" % par)

    issue_w1_dma(0, list(wbuf_free))
    issue_w2_dma(0, list(wbuf2_free) if wbuf2_free is not None else [])
    issue_w1_dma(1, [])
    issue_w2_dma(1, [])
    h_ev = {}
    y_ev = {0: None, 1: None}
    chunk_free = {}
    for i in range(nchunks):
        par = i % 2
        e_, q_ = i // nq, i % nq
        wtok = w_tok[i]
        last_q = (q_ == nq - 1)
        for tc in range(4):
            gl_toks = []
            for fs in range(nfs):
                bank = ps[fs % 2]
                for ct in range(NC):
                    waits = ()
                    if ct == 0:
                        waits = [wtok, lnc_tt[tc * 4 + 3], h_ev.get(fs % 2)]
                    b.emit("tensor", lambda e, bank=bank, ct=ct, tc=tc, fs=fs,
                           par=par: e.matmul(
                               bank[:, :],
                               w1b[:, par * w1w + ct * fchunk + fs * 128:
                                   par * w1w + ct * fchunk + fs * 128 + 128],
                               lnc[:, ct * S + tc * 512:ct * S + tc * 512 + 512],
                               start=(ct == 0), stop=(ct == NC - 1)),
                           waits=waits, inc=("pe" if ct == NC - 1 else None))
                t_mm = b.tok("pe")
                bias = b1_col(e_, q_, fs) if b1_col else 0.0
                h_ev[fs % 2] = b.emit("scalar", lambda e, fs=fs, bank=bank,
                                      bias=bias: e.activation(
                                          h[:, fs * 512:fs * 512 + 512],
                                          bank[:, :], AF.Gelu, bias=bias),
                                      waits=[t_mm], inc="act")
                gl_toks.append(h_ev[fs % 2])
            t_h = gl_toks[-1]
            if tc == 3 and i + 2 < nchunks:
                issue_w1_dma(i + 2, [b.tok("pe")])
            for ts in range(4):
                ta = tc * 4 + ts
                slot = ta % 2
                if slot == 0:
                    ybanks = [(psy, 0), (psy, 512)]
                else:
                    ybanks = [(ps[2], 0), (ps[3], 0)]
                nmm = nfs
                for dh in range(2):
                    yb, yo = ybanks[dh]
                    for fs in range(nfs):
                        waits = ()
                        if fs == 0:
                            waits = [t_h, y_ev[slot]]
                        b.emit("tensor", lambda e, yb=yb, yo=yo, fs=fs, ts=ts,
                               dh=dh, par=par: e.matmul(
                                   yb[:, yo:yo + 512],
                                   h[:, fs * 512 + ts * 128:fs * 512 + ts * 128 + 128],
                                   w2b[:, par * w2w + fs * 1024 + dh * 512:
                                       par * w2w + fs * 1024 + dh * 512 + 512],
                                   start=(fs == 0), stop=(fs == nfs - 1)),
                               waits=waits, inc=("pe" if fs == nfs - 1
                                                 else None))
                t_y = b.tok("pe")
                evs = []
                for dh in range(2):
                    yb, yo = ybanks[dh]
                    evs.append(b.emit("vector", lambda e, yb=yb, yo=yo, ta=ta,
                                      dh=dh, e_=e_: e.scalar_tensor_tensor(
                                          dst[:, ta * 1024 + dh * 512:
                                              ta * 1024 + dh * 512 + 512],
                                          yb[:, yo:yo + 512],
                                          mvec[:, ta * E + e_:ta * E + e_ + 1],
                                          dst[:, ta * 1024 + dh * 512:
                                              ta * 1024 + dh * 512 + 512],
                                          ALU.mult, ALU.add),
                                      waits=[t_y, t_mv] + list(dst_waits(ta)),
                                      inc="dve"))
                y_ev[slot] = evs[-1]
        chunk_free[i] = b.tok("pe")
        if i + 2 < nchunks:
            issue_w2_dma(i + 2, [chunk_free[i]])
    if b2_src is not None:
        t_de = b.tok("dve")
        bias_ev = {0: None, 1: None}
        for ta in range(NT):
            for dh in range(2):
                bank = ps[dh]
                t_bm = b.emit("tensor", lambda e, ta=ta, dh=dh, bank=bank:
                              e.matmul(bank[:, :],
                                       mvT[0:8, ta * 128:ta * 128 + 128],
                                       u["b2r"][0:8, dh * 512:dh * 512 + 512],
                                       start=True, stop=True),
                              waits=[t_b2all, mvT_tok[ta], h_ev.get(dh),
                                     bias_ev[dh]], inc="pe")
                bias_ev[dh] = b.emit("vector", lambda e, ta=ta, dh=dh,
                                     bank=bank: e.tensor_add(
                                         dst[:, ta * 1024 + dh * 512:
                                             ta * 1024 + dh * 512 + 512],
                                         dst[:, ta * 1024 + dh * 512:
                                             ta * 1024 + dh * 512 + 512],
                                         bank[:, :]),
                                     waits=[t_bm, t_de if ta == 0 else None],
                                     inc="dve")
    return b.tok("dve")


def alloc_moe(nc, stack, u, E=8, maxf=1024, weights=True):
    sb = lambda name, shape, dt: stack.enter_context(
        nc.sbuf_tensor("sb_" + name, shape, dt))
    if weights:
        u["w1b"] = sb("w1b", [128, 2 * NC * maxf], BF16)
        u["w2b"] = sb("w2b", [128, 2 * (maxf // 128) * 1024], BF16)
    u["logits"] = sb("logits", [128, NT * E], F32)
    u["mvec"] = sb("mvec", [128, NT * E], F32)
    u["sc8"] = sb("sc8", [128, 24], F32)
    u["router_sb"] = sb("router_sb", [128, NC * E], BF16)



def emit_conv(b, u, dr, norm_toks, scratch_free):
    ps, psy, pt = u["ps"], u["psy"], u["pt"]
    x, lnt = u["x"], u["lnt"]
    prm, tvals, bdiag = u["prm"], u["tvals"], u["bdiag"]
    e2f, e2m, ksb, kt = u["e2f"], u["e2m"], u["ksb"], u["kt"]
    zr, zi, wstr, pw, omgb = u["zr"], u["zi"], u["wstr"], u["pw"], u["omgb"]
    ident = u["ident"][:, :]

    C = lambda j: prm[:, j * 128:(j + 1) * 128]
    t_pl = b.dma(prm[:, 0:128], dr["delta"], sem="dpr", waits=list(scratch_free))
    b.dma(prm[:, 128:256], dr["alpha"], sem="dpr")
    b.dma(prm[:, 256:384], dr["beta"], sem="dpr")
    t_pl = b.dma(prm[:, 384:512], dr["gamma"], sem="dpr")
    t1 = b.emit("scalar", lambda e: e.activation(C(4), C(0), AF.Sigmoid),
                waits=[t_pl], inc="act")
    t2 = b.emit("scalar", lambda e: e.activation(C(5), C(1), AF.Sigmoid),
                waits=[t_pl], inc="act")
    t3 = b.emit("vector", lambda e: e.tensor_mul(C(5), C(4), C(5)),
                waits=[t2, b.tok("dve")], inc="dve")
    t4 = b.emit("vector", lambda e: e.tensor_scalar(C(5), C(5), -1.0, 1.0,
                                                    ALU.mult, ALU.add),
                waits=[t3], inc="dve")
    t_lq = b.emit("scalar", lambda e: e.activation(C(5), C(5), AF.Ln),
                  waits=[t4], inc="act")
    t5 = b.emit("vector", lambda e: e.tensor_mul(C(6), C(4), C(2)),
                waits=[t1, t_pl], inc="dve")
    t6 = b.emit("vector", lambda e: e.tensor_mul(C(6), C(6), C(3)),
                waits=[t5], inc="dve")
    t_c = b.emit("vector", lambda e: e.tensor_scalar_mul(C(6), C(6), 0.25),
                 waits=[t6], inc="dve")

    kev = {}
    e2_pe = {}
    t_exp = {}
    t_mul = {}
    for dt in range(128):
        db, j = dt // 16, dt % 16
        par = dt % 2
        waits = [t_lq] if dt == 0 else [t_mul[dt - 1]]
        waits = waits + [e2_pe.get(dt - 2)]
        t_exp[dt] = b.emit("scalar", lambda e, dt=dt, par=par: e.activation(
            e2m[:, par * 2048:par * 2048 + 2048], tvals[:, :], AF.Exp,
            scale=prm[:, 5 * 128 + dt:5 * 128 + dt + 1]),
            waits=waits, inc="act")
        t_mul[dt] = b.emit("vector", lambda e, dt=dt, par=par: e.tensor_scalar_mul(
            e2m[:, par * 2048:par * 2048 + 2048],
            e2m[:, par * 2048:par * 2048 + 2048],
            prm[:, 6 * 128 + dt:6 * 128 + dt + 1]),
            waits=[t_exp[dt], t_c], inc="dve")
        for lc in range(4):
            inc = None
            if lc == 3:
                inc = [("pe", 1)]
            b.emit("tensor", lambda e, dt=dt, j=j, lc=lc, par=par: e.matmul(
                ps[lc][:, :], bdiag[:, j * 128:(j + 1) * 128],
                e2m[:, par * 2048 + lc * 512:par * 2048 + lc * 512 + 512],
                start=(j == 0), stop=(j == 15)),
                waits=([t_mul[dt], kev.get(db - 1)] if lc == 0 else ()),
                inc=inc)
        e2_pe[dt] = b.tok("pe")
        if j == 15:
            t_mm = b.tok("pe")
            for lc in range(4):
                kev[db] = b.emit("scalar", lambda e, db=db, lc=lc: e.copy(
                    ksb[:, db * 2048 + lc * 512:db * 2048 + lc * 512 + 512],
                    ps[lc][:, :]), waits=[t_mm] if lc == 0 else (), inc="act")
    idx = 0
    tev = {}
    for db in range(8):
        for lt in range(16):
            off = (idx % 2) * 1024 + ((idx // 2) % 4) * 128
            waits = [kev[db]]
            if idx >= 2:
                waits.append(tev[idx - 2])
            if idx >= 8:
                waits.append(tev[idx - 8])
            t_tr = b.emit("tensor", lambda e, db=db, lt=lt, off=off: e.transpose(
                pt[:, off:off + 128],
                ksb[:, db * 2048 + lt * 128:db * 2048 + lt * 128 + 128], ident),
                waits=waits, inc="pe")
            tev[idx] = b.emit("scalar", lambda e, db=db, lt=lt, off=off: e.copy(
                kt[:, lt * 1024 + db * 128:lt * 1024 + db * 128 + 128],
                pt[:, off:off + 128]), waits=[t_tr], inc="act")
            idx += 1
    t_kt = tev[idx - 1]

    t_og = b.dma(omgb, dr["omegab"], waits=[b.tok("pe")], sem="dog")

    inv_bank_ev = {}
    inv_free = []
    xtile_toks = [None] * NT
    for dh in range(2):
        str_tok = {}
        str_free = {}

        def issue_strip(ft, waits, dh=dh):
            par = ft % 2
            str_tok[ft] = b.dma(
                sb3(wstr[:, par * 4096:par * 4096 + 4096], 32),
                dr["dftW"][ft].rearrange("cs lt p f -> p (cs lt) f"),
                waits=waits, sem="dw%d" % par)

        issue_strip(0, [b.tok("pe")] + (list(scratch_free) if dh == 0 else []))
        issue_strip(1, [])
        pw_ev = None
        z_toks = []
        evt_last = None
        for ft in range(NFT):
            par = ft % 2
            wb = par * 4096
            for lt in range(16):
                for ci in range(2):
                    for xk in range(2):
                        bank = ps[ci * 2 + xk]
                        rhs = (lnt if xk == 0 else kt)
                        waits = ()
                        if lt == 0:
                            waits = [str_tok[ft], t_kt]
                            if evt_last is not None:
                                waits.append(evt_last)
                            else:
                                waits += inv_free
                            if dh == 0 and ft == 0:
                                waits = list(waits) + [norm_toks[-1]]
                        b.emit("tensor", lambda e, bank=bank, ci=ci, lt=lt,
                               wb=wb, rhs=rhs, dh=dh: e.matmul(
                                   bank[:, :],
                                   wstr[:, wb + ci * 2048 + lt * 128:
                                        wb + ci * 2048 + lt * 128 + 128],
                                   rhs[:, lt * 1024 + dh * 512:
                                       lt * 1024 + dh * 512 + 512],
                                   start=(lt == 0), stop=(lt == 15)),
                               waits=waits,
                               inc=("pe" if lt == 15 and ci == 1 and xk == 1
                                    else None))
            t_f = b.tok("pe")
            if ft + 2 < NFT:
                issue_strip(ft + 2, [t_f])
            evt = []
            for q in range(4):
                w_ev = ()
                if q == 0:
                    w_ev = [t_f] + ([pw_ev] if pw_ev else [])
                evt.append(b.emit("scalar", lambda e, q=q: e.copy(
                    pw[:, q * 512:q * 512 + 512], ps[q][:, :]),
                    waits=w_ev, inc="act"))
            evt_last = evt[-1]
            P0 = lambda q: pw[:, q * 512:(q + 1) * 512]
            w0 = [evt[-1], pw_ev] if pw_ev else [evt[-1]]
            a1 = b.emit("vector", lambda e: e.tensor_mul(
                pw[:, 2048:2560], P0(0), P0(1)), waits=w0 + [b.tok("dve")],
                inc="dve")
            a2 = b.emit("vector", lambda e: e.tensor_mul(
                pw[:, 2560:3072], P0(2), P0(3)), waits=[a1], inc="dve")
            a3 = b.emit("vector", lambda e, ft=ft: e.tensor_sub(
                zr[:, ft * 512:ft * 512 + 512],
                pw[:, 2048:2560], pw[:, 2560:3072]), waits=[a2], inc="dve")
            a4 = b.emit("vector", lambda e: e.tensor_mul(
                pw[:, 2048:2560], P0(0), P0(3)), waits=[a3], inc="dve")
            a5 = b.emit("vector", lambda e: e.tensor_mul(
                pw[:, 2560:3072], P0(2), P0(1)), waits=[a4], inc="dve")
            a6 = b.emit("vector", lambda e, ft=ft: e.tensor_add(
                zi[:, ft * 512:ft * 512 + 512],
                pw[:, 2048:2560], pw[:, 2560:3072]), waits=[a5], inc="dve")
            pw_ev = a6
            z_toks.append(a6)
        t_z = z_toks[-1]
        cs_tok = {}

        def issue_cstrip(ft, waits, dh=dh):
            par = ft % 2
            cs_tok[ft] = b.dma(
                sb3(wstr[:, par * 4096:par * 4096 + 4096], 32),
                dr["dftC"][ft].rearrange("cs lt p f -> p (cs lt) f"),
                waits=waits, sem="dwc%d" % par)
        ptf = u["ptf"]
        ibanks = [psy[:, 0:512], psy[:, 512:1024], ps[0][:, :], ps[1][:, :],
                  ps[2][:, :], ps[3][:, :], ptf[:, 0:512], ptf[:, 512:1024]]
        lgs = [(0, 8), (8, 8)]
        for lg, (lbase, lcnt) in enumerate(lgs):
            issue_cstrip(0, [b.tok("pe")])
            issue_cstrip(1, [])
            for ft in range(NFT):
                for li in range(lcnt):
                    lt = lbase + li
                    bank = ibanks[li]
                    for ci in range(2):
                        zb = zr if ci == 0 else zi
                        waits = ()
                        if ft == 0 and ci == 0:
                            waits = [t_z, cs_tok[0], inv_bank_ev.get(li)]
                        elif li == 0 and ci == 0:
                            waits = [cs_tok[ft]]
                        b.emit("tensor", lambda e, bank=bank, lt=lt, ft=ft,
                               ci=ci, zb=zb: e.matmul(
                                   bank,
                                   wstr[:, (ft % 2) * 4096 + ci * 2048 +
                                        (lt % 16) * 128:
                                        (ft % 2) * 4096 + ci * 2048 +
                                        (lt % 16) * 128 + 128],
                                   zb[:, ft * 512:ft * 512 + 512],
                                   start=(ft == 0 and ci == 0),
                                   stop=(ft == NFT - 1 and ci == 1)),
                               waits=waits,
                               inc=("pe" if ci == 1 and li == lcnt - 1
                                    else None))
                if ft + 2 < NFT:
                    issue_cstrip(ft + 2, [b.tok("pe")])
            t_inv = b.tok("pe")
            r_last = {}
            for li in range(lcnt):
                lt = lbase + li
                so = (li % 2) * 512
                bank = ibanks[li]
                xs = x[:, lt * 1024 + dh * 512:lt * 1024 + dh * 512 + 512]
                r1 = b.emit("vector", lambda e, lt=lt, dh=dh, so=so: e.tensor_mul(
                    u["sq"][:, so:so + 512],
                    lnt[:, lt * 1024 + dh * 512:lt * 1024 + dh * 512 + 512],
                    omgb[:, dh * 512:dh * 512 + 512]),
                    waits=[t_inv if li < 2 else None, r_last.get(li - 2),
                           t_og], inc="dve")
                r2 = b.emit("vector", lambda e, xs=xs, bank=bank: e.tensor_add(
                    xs, xs, bank), waits=[r1, norm_toks[lt]], inc="dve")
                r_last[li] = b.emit("vector", lambda e, xs=xs, so=so: e.tensor_add(
                    xs, xs, u["sq"][:, so:so + 512]), waits=[r2], inc="dve")
                inv_bank_ev[li] = r2
                if dh == 1:
                    xtile_toks[lt] = r_last[li]
            inv_free = [r_last[lcnt - 2], r_last[lcnt - 1]]
    return b.tok("dve"), xtile_toks


def alloc_conv(nc, stack, u):
    sbl = lambda name, shape, dt: stack.enter_context(
        nc.sbuf_tensor("sb_" + name, shape, dt))
    u["prm"] = sbl("prm", [128, 8 * 128], F32)
    u["bdiag"] = sbl("bdiag", [128, 2048], BF16)
    big1 = sbl("big1", [128, 2 * NFT * 512], BF16)
    u["ksb"] = big1[:, 0:16384]
    u["zr"] = big1[:, 0:NFT * 512]
    u["zi"] = big1[:, NFT * 512:2 * NFT * 512]
    big2 = sbl("big2", [128, 8192], BF16)
    u["e2f"] = big2.bitcast(F32)[:, 0:2048]
    u["e2m"] = big2[:, 4096:8192]
    u["wstr"] = big2[:, 0:8192]
    big3 = sbl("big3", [128, 6 * 512], F32)
    u["tvals"] = big3[:, 0:2048]
    u["pw"] = big3[:, 0:3072]
    u["kt"] = sbl("kt", [128, 16384], BF16)
    u["omgb"] = sbl("omgb", [128, 1024], F32)[:, :]


def build_p2(P):
    def bf(nc, stack, b):
        xd = nc.dram_tensor("x", [S, DM], F32, kind="ExternalInput").ap()
        dd = nc.dram_tensor("delta", [128, 128], F32, kind="ExternalInput").ap()
        ad = nc.dram_tensor("alphap", [128, 128], F32, kind="ExternalInput").ap()
        bd = nc.dram_tensor("betap", [128, 128], F32, kind="ExternalInput").ap()
        gd = nc.dram_tensor("gammap", [128, 128], F32, kind="ExternalInput").ap()
        od = nc.dram_tensor("omegab", [128, 1024], F32, kind="ExternalInput").ap()
        td = nc.dram_tensor("tvalsc", [128, 2048], F32, kind="ExternalInput").ap()
        bdd = nc.dram_tensor("bdiagc", [128, 2048], F32, kind="ExternalInput").ap()
        wfd = nc.dram_tensor("dftW", [NFT, 2, 16, 128, 128], BF16,
                             kind="ExternalInput").ap()
        cvd = nc.dram_tensor("dftC", [NFT, 2, 16, 128, 128], BF16,
                             kind="ExternalInput").ap()
        idd = nc.dram_tensor("identc", [128, 128], F32, kind="ExternalInput").ap()
        fcd = nc.dram_tensor("fconstc", [128, 8], F32, kind="ExternalInput").ap()
        xo = nc.dram_tensor("xo", [S, DM], F32, kind="ExternalOutput").ap()

        u = alloc_common(nc, stack)
        sbl = lambda name, shape, dt: stack.enter_context(
            nc.sbuf_tensor("sb_" + name, shape, dt))
        u["lnt"] = sbl("lnt", [128, NT * DM], BF16)
        alloc_conv(nc, stack, u)

        b.dma(u["ident"][:, :], idd, sem="dc")
        b.dma(u["fc"][:, :], fcd, sem="dc")
        b.dma(u["tvals"][:, :], td, sem="dc")
        b.dma(u["bdiag"][:, :], bdd, sem="dc")
        t_c = b.tok("dc")
        t_x = b.dma(sb3(u["x"][:, :], NT), xd.rearrange("(t p) d -> p t d", p=128),
                    eng="sync", sem="ds")
        b.wait_only("vector", [t_x, t_c])
        b.wait_only("scalar", [t_x, t_c])
        b.wait_only("tensor", [t_c])
        norm_toks = emit_ln(b, u["x"][:, :], u["sq"], u["st"],
                            lambda ti: u["lnt"][:, ti * DM:(ti + 1) * DM],
                            u["fc"][:, 0:1])
        t_all, _ = emit_conv(b, u, {"delta": dd, "alpha": ad, "beta": bd,
                                    "gamma": gd, "dftW": wfd, "dftC": cvd,
                                    "omegab": od},
                             norm_toks, scratch_free=[])
        t_st = b.dma(xo.rearrange("(t p) d -> p t d", p=128), sb3(u["x"][:, :], NT),
                     waits=[t_all], eng="sync", sem="ds")
        b.wait_only("sync", [t_st])
    return build_nc(bf)


def p2_maps(P, c):
    return {"x": None, "delta": P["l1_delta"], "alphap": P["l1_alpha"],
            "betap": P["l1_beta"], "gammap": P["l1_gamma"],
            "omegab": P["l1_omega_b"], "tvalsc": P["tvals"],
            "bdiagc": P["bdiag"], "dftW": P["dftW"], "dftC": P["dftC"],
            "identc": P["ident"], "fconstc": P["fconst"]}



def alloc_common(nc, stack, with_x=True):
    sb = lambda name, shape, dt: stack.enter_context(
        nc.sbuf_tensor("sb_" + name, shape, dt))
    u = {}
    if with_x:
        u["x"] = sb("x", [128, NT * DM], F32)
    u["sq"] = sb("sq", [128, DM], F32)
    u["st"] = sb("st", [128, 16], F32)
    u["ident"] = sb("ident", [128, 128], BF16)
    u["ones"] = sb("ones", [128, 128], BF16)
    u["fc"] = sb("fc", [128, 8], F32)
    u["psA"] = stack.enter_context(nc.psum_tensor("psA", [128, 1024], F32))
    u["psB"] = stack.enter_context(nc.psum_tensor("psB", [128, 1024], F32))
    u["ps"] = [u["psA"][:, 0:512], u["psA"][:, 512:1024],
               u["psB"][:, 0:512], u["psB"][:, 512:1024]]
    u["psy"] = stack.enter_context(nc.psum_tensor("psy", [128, 1024], F32))
    u["ptb"] = stack.enter_context(nc.psum_tensor("ptb", [128, 2048], BF16))
    u["pt"] = u["ptb"]
    u["ptf"] = u["ptb"].bitcast(F32)
    return u


def load_common_consts(b, u, dr):
    b.dma(u["ident"][:, :], dr["ident"])
    b.dma(u["ones"][:, :], dr["ones"])


def alloc_attn(nc, stack, u):
    sb = lambda name, shape, dt: stack.enter_context(
        nc.sbuf_tensor("sb_" + name, shape, dt))
    u["lnt"] = sb("lnt", [128, NT * DM], BF16)
    u["lnc"] = sb("lnc", [128, NC * S], BF16)
    u["qT"] = sb("qT", [128, S], BF16)
    u["kT"] = sb("kT", [128, S], BF16)
    u["vON"] = sb("vON", [128, NT * 192], BF16)
    u["probsT"] = sb("probsT", [128, 16 * 512], BF16)
    u["aoT"] = u["lnt"]
    u["wqkv"] = sb("wqkv", [128, 2 * 3 * 1024], BF16)
    u["wo"] = sb("wo", [128, 8 * 512], BF16)
    u["invr"] = sb("invr", [128, 512], BF16)
    u["tmp64"] = sb("tmp64", [128, 512], BF16)
    u["cmask"] = sb("cmask", [128, 2048], BF16)


def emit_attn_block(b, u, dr_w, wbuf_free=(), x_free=(), scratch_free=()):
    norm_toks = emit_ln(b, u["x"][:, :], u["sq"], u["st"],
                        lambda ti: u["lnt"][:, ti * DM:(ti + 1) * DM],
                        u["fc"][:, 0:1], norm_free=lambda ti: list(x_free))
    lnc_tok, tp_tok, lnc_tt = emit_lnt_to_lnc(b, u["lnt"], u["lnc"], u["pt"],
                                              u["ident"][:, :], norm_toks)
    w_srcs = lambda g: (cmaj(dr_w["q"], 0, DM, g * 128, (g + 1) * 128),
                        cmaj(dr_w["k"], 0, DM, g * 128, (g + 1) * 128),
                        cmaj(dr_w["v"], 0, DM, g * 128, (g + 1) * 128))
    wo_src = lambda dh: cmaj(dr_w["o"], 0, DM, dh * 512, (dh + 1) * 512)
    return emit_attention(b, u, u["lnc"], u["x"], w_srcs, wo_src,
                          lnc_tok, [tp_tok], lambda tt: [norm_toks[tt]],
                          wbuf_free, list(scratch_free), lnc_tt=lnc_tt)[0]



def build_p0(P):
    def bf(nc, stack, b):
        xd = nc.dram_tensor("x", [S, DM], F32, kind="ExternalInput").ap()
        qd = nc.dram_tensor("qT", [DM, DM], F32, kind="ExternalInput").ap()
        kd = nc.dram_tensor("kT", [DM, DM], F32, kind="ExternalInput").ap()
        vd = nc.dram_tensor("vT", [DM, DM], F32, kind="ExternalInput").ap()
        od = nc.dram_tensor("oT", [DM, DM], F32, kind="ExternalInput").ap()
        idd = nc.dram_tensor("identc", [128, 128], F32, kind="ExternalInput").ap()
        oned = nc.dram_tensor("onesc", [128, 128], F32, kind="ExternalInput").ap()
        cmd = nc.dram_tensor("cmaskc", [128, 2048], F32, kind="ExternalInput").ap()
        xo = nc.dram_tensor("xo", [S, DM], F32, kind="ExternalOutput").ap()

        u = alloc_common(nc, stack)
        alloc_attn(nc, stack, u)
        fcd = nc.dram_tensor("fconstc", [128, 8], F32, kind="ExternalInput").ap()
        b.dma(u["ident"][:, :], idd, sem="dc")
        b.dma(u["ones"][:, :], oned, sem="dc")
        b.dma(u["cmask"][:, :], cmd, sem="dc")
        b.dma(u["fc"][:, :], fcd, sem="dc")
        t_c = b.tok("dc")
        t_x = b.dma(sb3(u["x"][:, :], NT), xd.rearrange("(t p) d -> p t d", p=128),
                    eng="sync", sem="ds")
        b.wait_only("vector", [t_x, t_c])
        b.wait_only("scalar", [t_x, t_c])
        b.wait_only("tensor", [t_c])
        t_all = emit_attn_block(b, u, {"q": qd, "k": kd, "v": vd, "o": od})
        t_st = b.dma(xo.rearrange("(t p) d -> p t d", p=128), sb3(u["x"][:, :], NT),
                     waits=[t_all], eng="sync", sem="ds")
        b.wait_only("sync", [t_st])
    return build_nc(bf)


def emit_ln_lnc(b, u, x_free=(), wrow=None):
    norm_toks = emit_ln(b, u["x"][:, :], u["sq"], u["st"],
                        lambda ti: u["lnt"][:, ti * DM:(ti + 1) * DM],
                        u["fc"][:, 0:1], norm_free=lambda ti: list(x_free))
    lnc_tok, tp_tok, lnc_tt = emit_lnt_to_lnc(b, u["lnt"], u["lnc"], u["pt"],
                                              u["ident"][:, :], norm_toks)
    return norm_toks, lnc_tok, tp_tok, lnc_tt


def build_p1(P):
    def bf(nc, stack, b):
        xd = nc.dram_tensor("x", [S, DM], F32, kind="ExternalInput").ap()
        w1d = nc.dram_tensor("w1t", [8, DM, 4096], F32, kind="ExternalInput").ap()
        w2d = nc.dram_tensor("w2t", [8, 4096, DM], F32, kind="ExternalInput").ap()
        b1d = nc.dram_tensor("b1x", [128, 256], F32, kind="ExternalInput").ap()
        b2d = nc.dram_tensor("b2r", [1, 8192], F32, kind="ExternalInput").ap()
        rtd = nc.dram_tensor("routerT", [DM, 8], F32, kind="ExternalInput").ap()
        idd = nc.dram_tensor("identc", [128, 128], F32, kind="ExternalInput").ap()
        oned = nc.dram_tensor("onesc", [128, 128], F32, kind="ExternalInput").ap()
        fcd = nc.dram_tensor("fconstc", [128, 8], F32, kind="ExternalInput").ap()
        xo = nc.dram_tensor("xo", [S, DM], F32, kind="ExternalOutput").ap()

        u = alloc_common(nc, stack)
        sbl = lambda name, shape, dt: stack.enter_context(
            nc.sbuf_tensor("sb_" + name, shape, dt))
        u["lnt"] = sbl("lnt", [128, NT * DM], BF16)
        u["lnc"] = sbl("lnc", [128, NC * S], BF16)
        u["b1x"] = sbl("b1x", [128, 256], F32)
        u["b2r"] = sbl("b2r", [8, 1024], BF16)
        alloc_moe(nc, stack, u, E=8, maxf=1024)
        u["h"] = u["lnt"]

        b.dma(u["ident"][:, :], idd, sem="dc")
        b.dma(u["ones"][:, :], oned, sem="dc")
        b.dma(u["fc"][:, :], fcd, sem="dc")
        b.dma(u["b1x"][:, :], b1d, sem="dc")
        t_c = b.tok("dc")
        t_x = b.dma(sb3(u["x"][:, :], NT), xd.rearrange("(t p) d -> p t d", p=128),
                    eng="sync", sem="ds")
        b.wait_only("vector", [t_x, t_c])
        b.wait_only("scalar", [t_x, t_c])
        b.wait_only("tensor", [t_c])
        norm_toks, lnc_tok, tp_tok, lnc_tt = emit_ln_lnc(b, u)
        t_all = emit_moe(
            b, u, u["lnc"], u["x"], E=8, F=4096, fchunk=1024,
            w1_src=lambda e, q: cmaj(w1d[e], 0, DM, q * 1024, (q + 1) * 1024),
            w2_src=lambda e, q: cmaj(w2d[e], q * 1024, (q + 1) * 1024, 0, DM),
            router_src=cmaj(rtd, 0, DM, 0, 8),
            b1_col=lambda e, q, fs: u["b1x"][:, e * 32 + q * 8 + fs:
                                             e * 32 + q * 8 + fs + 1],
            b2_src=b2d.rearrange("o (e h) -> (o e) h", e=8),
            lnc_tok=lnc_tok, dst_waits=lambda ta: [norm_toks[ta]],
            scratch_free=[tp_tok], wbuf_free=[], lnc_tt=lnc_tt)
        t_st = b.dma(xo.rearrange("(t p) d -> p t d", p=128), sb3(u["x"][:, :], NT),
                     waits=[t_all], eng="sync", sem="ds")
        b.wait_only("sync", [t_st])
    return build_nc(bf)


def p1_maps(P, c):
    return {"x": None, "w1t": P["l0_w1t"], "w2t": P["l0_w2t"],
            "b1x": P["l0_b1"], "b2r": P["l0_b2"], "routerT": P["l0_routerT"],
            "identc": P["ident"], "onesc": P["ones"], "fconstc": P["fconst"]}


def build_p3(P):
    def bf(nc, stack, b):
        xd = nc.dram_tensor("x", [S, DM], F32, kind="ExternalInput").ap()
        cad = nc.dram_tensor("cattn", [DM, 3 * DM], F32, kind="ExternalInput").ap()
        cpd = nc.dram_tensor("cproj", [DM, DM], F32, kind="ExternalInput").ap()
        l1d = nc.dram_tensor("ln1b", [128, DM], F32, kind="ExternalInput").ap()
        l2d = nc.dram_tensor("ln2b", [128, DM], F32, kind="ExternalInput").ap()
        rtd = nc.dram_tensor("router", [DM, 8], F32, kind="ExternalInput").ap()
        w1d = nc.dram_tensor("w1", [DM, 16384], F32, kind="ExternalInput").ap()
        w2d = nc.dram_tensor("w2", [16384, DM], F32, kind="ExternalInput").ap()
        idd = nc.dram_tensor("identc", [128, 128], F32, kind="ExternalInput").ap()
        oned = nc.dram_tensor("onesc", [128, 128], F32, kind="ExternalInput").ap()
        cmd = nc.dram_tensor("cmaskc", [128, 2048], F32, kind="ExternalInput").ap()
        fcd = nc.dram_tensor("fconstc", [128, 8], F32, kind="ExternalInput").ap()
        xo = nc.dram_tensor("xo", [S, DM], F32, kind="ExternalOutput").ap()

        u = alloc_common(nc, stack)
        sbl = lambda name, shape, dt: stack.enter_context(
            nc.sbuf_tensor("sb_" + name, shape, dt))
        u["lnt"] = sbl("lnt", [128, NT * DM], BF16)
        u["lnc"] = sbl("lnc", [128, NC * S], BF16)
        u["lnw"] = sbl("lnw", [128, 2 * DM], F32)
        u["lnw1"] = u["lnw"][:, 0:DM]
        u["lnw2"] = u["lnw"][:, DM:2 * DM]
        A3 = sbl("A3", [128, 32768], BF16)
        u["qT"] = A3[:, 0:2048]
        u["kT"] = A3[:, 2048:4096]
        u["vON"] = A3[:, 4096:7168]
        u["probsT"] = A3[:, 7168:15360]
        u["wqkv"] = A3[:, 15360:21504]
        u["wo"] = A3[:, 21504:25600]
        u["invr"] = A3[:, 25600:26112]
        u["tmp64"] = A3[:, 26112:26624]
        u["cmask"] = A3[:, 26624:28672]
        u["aoT"] = u["lnt"]
        u["w1b"] = A3[:, 0:16384]
        u["w2b"] = A3[:, 16384:32768]
        u["xf"] = A3.bitcast(F32)
        u["h"] = u["lnt"]
        alloc_moe(nc, stack, u, E=8, maxf=1024, weights=False)

        b.dma(u["ident"][:, :], idd, sem="dc")
        b.dma(u["ones"][:, :], oned, sem="dc")
        b.dma(u["fc"][:, :], fcd, sem="dc")
        b.dma(u["cmask"], cmd, sem="dc")
        b.dma(u["lnw"][:, 0:DM], l1d, sem="dc")
        b.dma(u["lnw"][:, DM:2 * DM], l2d, sem="dc")
        t_c = b.tok("dc")
        t_x = b.dma(sb3(u["x"][:, :], NT), xd.rearrange("(t p) d -> p t d", p=128),
                    eng="sync", sem="ds")
        b.wait_only("vector", [t_x, t_c])
        b.wait_only("scalar", [t_x, t_c])
        b.wait_only("tensor", [t_c])
        emit_l1ffn(b, u, {"cattn": cad, "cproj": cpd, "router": rtd,
                          "w1": w1d, "w2": w2d}, xd, xo)
    return build_nc(bf)


def emit_l1ffn(b, u, dr, x_src_hbm, x_dst_hbm, phase_free=(), spilled=False,
               xtiles=None, sq_early=None):
    sqe = sq_early if sq_early is not None else u["sq"]
    nt0 = emit_ln(b, u["x"][:, :], sqe, u["st"],
                  lambda ti: u["x"][:, ti * DM:(ti + 1) * DM], u["fc"][:, 0:1],
                  norm_free=lambda ti: list(phase_free) +
                  ([("dsp%d" % (ti // 4), 16)] if spilled else []),
                  x_ready=(lambda ti: [xtiles[ti]]) if xtiles else None)
    nt1 = emit_ln(b, u["x"][:, :], sqe, u["st"],
                  lambda ti: u["lnt"][:, ti * DM:(ti + 1) * DM], u["fc"][:, 0:1],
                  wrow=u["lnw1"],
                  x_ready=lambda ti: [nt0[ti]] + ([nt0[-1]] if ti == 0 else []))
    lnc_tok, tp_tok, lnc_tt = emit_lnt_to_lnc(b, u["lnt"], u["lnc"], u["pt"],
                                              u["ident"][:, :], nt1,
                                              extra_wait=list(phase_free))
    w_srcs = lambda g: (cmaj(dr["cattn"], 0, DM, g * 128, (g + 1) * 128),
                        cmaj(dr["cattn"], 0, DM, DM + g * 128, DM + (g + 1) * 128),
                        cmaj(dr["cattn"], 0, DM, 2 * DM + g * 128,
                             2 * DM + (g + 1) * 128))
    wo_src = lambda dh: cmaj(dr["cproj"], 0, DM, dh * 512, (dh + 1) * 512)
    t_attn, t_heads = emit_attention(b, u, u["lnc"], u["x"], w_srcs, wo_src,
                                     lnc_tok, [tp_tok], lambda tt: [nt1[tt]],
                                     list(phase_free), list(phase_free),
                                     lnc_tt=lnc_tt)
    nt2 = emit_ln(b, u["x"][:, :], u["sq"], u["st"],
                  lambda ti: u["lnt"][:, ti * DM:(ti + 1) * DM], u["fc"][:, 0:1],
                  wrow=u["lnw2"], x_ready=lambda ti: [t_attn],
                  norm_free=lambda ti: [t_attn])
    lnc_tok2, tp_tok2, lnc_tt2 = emit_lnt_to_lnc(b, u["lnt"], u["lnc"],
                                                 u["pt"], u["ident"][:, :],
                                                 nt2, extra_wait=[t_attn])
    t_moe = emit_moe(
        b, u, u["lnc"], u["x"], E=8, F=2048, fchunk=1024,
        w1_src=lambda e, q: cmaj(dr["w1"], 0, DM, e * 2048 + q * 1024,
                                 e * 2048 + (q + 1) * 1024),
        w2_src=lambda e, q: cmaj(dr["w2"], e * 2048 + q * 1024,
                                 e * 2048 + (q + 1) * 1024, 0, DM),
        router_src=cmaj(dr["router"], 0, DM, 0, 8),
        b1_col=None, b2_src=None,
        lnc_tok=lnc_tok2, dst_waits=lambda ta: [nt2[ta]],
        scratch_free=[tp_tok2, t_attn], wbuf_free=[t_heads], lnc_tt=lnc_tt2,
        wbuf2_free=[t_attn, nt2[-1]])
    src3 = x_src_hbm.rearrange("(t p) d -> p t d", p=128)
    dst3 = x_dst_hbm.rearrange("(t p) d -> p t d", p=128)
    xb3 = sb3(u["x"][:, :], NT)
    for qq in range(4):
        t_acc = b.emit("gpsimd", lambda e, qq=qq: e.dma_start(
            xb3[:, qq * 4:(qq + 1) * 4, :], src3[:, qq * 4:(qq + 1) * 4, :],
            accum_op=ALU.add),
            waits=[t_moe] if qq == 0 else (), inc="dac%d" % qq, inc_amt=16)
        b.dma(dst3[:, qq * 4:(qq + 1) * 4, :], xb3[:, qq * 4:(qq + 1) * 4, :],
              waits=[t_acc], eng="sync", sem="ds")
    t_st = b.tok("ds")
    b.wait_only("sync", [t_st])
    return t_st


def p3_maps(P, c):
    return {"x": None, "cattn": P["l1_cattn"], "cproj": P["l1_cproj"],
            "ln1b": P["l1_ln1_b"], "ln2b": P["l1_ln2_b"],
            "router": P["l1_router"], "w1": P["l1_w1"], "w2": P["l1_w2"],
            "identc": P["ident"], "onesc": P["ones"], "cmaskc": P["cmask"],
            "fconstc": P["fconst"]}


def build_fused(P):
    def bf(nc, stack, b):
        dt_in = lambda n, shp: nc.dram_tensor(n, shp, F32, kind="ExternalInput").ap()
        xd = dt_in("x", [S, DM])
        qd, kd, vd, od = (dt_in("qT", [DM, DM]), dt_in("kT", [DM, DM]),
                          dt_in("vT", [DM, DM]), dt_in("oT", [DM, DM]))
        w1d0 = dt_in("w1t", [8, DM, 4096])
        w2d0 = dt_in("w2t", [8, 4096, DM])
        b1d = dt_in("b1x", [128, 256])
        b2d = dt_in("b2r", [1, 8192])
        rtd0 = dt_in("routerT", [DM, 8])
        dd, ad_, bd_, gd = (dt_in("delta", [128, 128]), dt_in("alphap", [128, 128]),
                            dt_in("betap", [128, 128]), dt_in("gammap", [128, 128]))
        odb = dt_in("omegab", [128, DM])
        td = dt_in("tvalsc", [128, 2048])
        bdd = dt_in("bdiagc", [128, 2048])
        wfd = nc.dram_tensor("dftW", [NFT, 2, 16, 128, 128], BF16,
                             kind="ExternalInput").ap()
        cvd = nc.dram_tensor("dftC", [NFT, 2, 16, 128, 128], BF16,
                             kind="ExternalInput").ap()
        cad = dt_in("cattn", [DM, 3 * DM])
        cpd = dt_in("cproj", [DM, DM])
        l1d = dt_in("ln1b", [128, DM])
        l2d = dt_in("ln2b", [128, DM])
        rtd1 = dt_in("router", [DM, 8])
        w1d1 = dt_in("w1", [DM, 16384])
        w2d1 = dt_in("w2", [16384, DM])
        idd = dt_in("identc", [128, 128])
        oned = dt_in("onesc", [128, 128])
        cmd = dt_in("cmaskc", [128, 2048])
        fcd = dt_in("fconstc", [128, 8])
        xo = nc.dram_tensor("xo", [S, DM], F32, kind="ExternalOutput").ap()
        xsp = nc.dram_tensor("xspill", [S, DM], F32, kind="Internal").ap()

        u = alloc_common(nc, stack)
        sbl = lambda name, shape, dt: stack.enter_context(
            nc.sbuf_tensor("sb_" + name, shape, dt))
        AR = sbl("AR", [128, 68608], BF16)
        ARF = AR.bitcast(F32)
        u["lnt"] = AR[:, 0:16384]
        u["lnc"] = AR[:, 16384:32768]
        u["lnw1"] = ARF[:, 32768:33792]
        u["lnw2"] = ARF[:, 31744:32768]
        u["b1x"] = sbl("b1x", [128, 256], F32)
        u["b2r"] = sbl("b2r", [8, 1024], BF16)
        alloc_moe(nc, stack, u, weights=False)
        at = {"qT": AR[:, 32768:34816], "kT": AR[:, 34816:36864],
              "vON": AR[:, 36864:39936], "probsT": AR[:, 39936:48128],
              "wqkv": AR[:, 48128:54272], "wo": AR[:, 54272:58368],
              "invr": AR[:, 58368:58880], "tmp64": AR[:, 58880:59392],
              "cmask": AR[:, 59392:61440], "aoT": u["lnt"]}
        f0 = {"w1b": AR[:, 32768:49152], "w2b": AR[:, 49152:65536],
              "h": u["lnt"]}
        cv = {"kt": AR[:, 16384:32768], "ksb": AR[:, 32768:49152],
              "zr": AR[:, 32768:41472], "zi": AR[:, 41472:50176],
              "wstr": AR[:, 50176:58368],
              "e2f": ARF[:, 25088:27136],
              "e2m": AR[:, 54272:58368],
              "tvals": ARF[:, 29184:31232],
              "pw": ARF[:, 29184:32256],
              "prm": ARF[:, 32256:33280],
              "bdiag": AR[:, 66560:68608],
              "omgb": ARF[:, 33280:34304]}
        u.update(at)

        b.dma(u["ident"][:, :], idd, sem="dc")
        b.dma(u["ones"][:, :], oned, sem="dc")
        b.dma(u["fc"][:, :], fcd, sem="dc")
        b.dma(u["cmask"], cmd, sem="dc")
        b.dma(u["b1x"][:, :], b1d, sem="dc")
        t_c = b.tok("dc")
        t_x = b.dma(sb3(u["x"][:, :], NT), xd.rearrange("(t p) d -> p t d", p=128),
                    eng="sync", sem="ds")
        b.wait_only("vector", [t_x, t_c])
        b.wait_only("scalar", [t_x, t_c])
        b.wait_only("tensor", [t_c])

        emit_attn_block(b, u, {"q": qd, "k": kd, "v": vd, "o": od})
        b.join(("tensor", "scalar", "vector", "gpsimd", "sync"))
        u1 = dict(u); u1.update(f0)
        norm_toks, lnc_tok, tp_tok, lnc_tt = emit_ln_lnc(b, u1)
        emit_moe(
            b, u1, u1["lnc"], u1["x"], E=8, F=4096, fchunk=1024,
            w1_src=lambda e, q: cmaj(w1d0[e], 0, DM, q * 1024, (q + 1) * 1024),
            w2_src=lambda e, q: cmaj(w2d0[e], q * 1024, (q + 1) * 1024, 0, DM),
            router_src=cmaj(rtd0, 0, DM, 0, 8),
            b1_col=lambda e, q, fs: u1["b1x"][:, e * 32 + q * 8 + fs:
                                              e * 32 + q * 8 + fs + 1],
            b2_src=b2d.rearrange("o (e h) -> (o e) h", e=8),
            lnc_tok=lnc_tok, dst_waits=lambda ta: [norm_toks[ta]],
            scratch_free=[tp_tok], wbuf_free=[], lnc_tt=lnc_tt)
        b.join(("tensor", "scalar", "vector", "gpsimd", "sync"))
        u2 = dict(u); u2.update(cv)
        b.dma(u2["tvals"], td, sem="dc")
        b.dma(u2["bdiag"], bdd, sem="dc")
        t_c2 = b.tok("dc")
        b.wait_only("scalar", [t_c2])
        b.wait_only("tensor", [t_c2])
        nt_c = emit_ln(b, u2["x"][:, :], u2["sq"], u2["st"],
                       lambda ti: u2["lnt"][:, ti * DM:(ti + 1) * DM],
                       u2["fc"][:, 0:1])
        t_cv, xtiles = emit_conv(b, u2, {"delta": dd, "alpha": ad_,
                                         "beta": bd_, "gamma": gd,
                                         "dftW": wfd, "dftC": cvd,
                                         "omegab": odb},
                                 nt_c, scratch_free=[])
        b.join(("tensor", "gpsimd"))
        xsp3 = xsp.rearrange("(t p) d -> p t d", p=128)
        for qq in range(4):
            b.dma(xsp3[:, qq * 4:(qq + 1) * 4, :],
                  sb3(u["x"][:, :], NT)[:, qq * 4:(qq + 1) * 4, :],
                  eng="sync", sem="dsp%d" % qq,
                  waits=[xtiles[qq * 4 + 3]])
        t_sp = None
        u3 = dict(u); u3.update(at)
        u3["w1b"] = AR[:, 32768:49152]
        u3["w2b"] = AR[:, 49152:65536]
        u3["h"] = u3["lnt"]
        u3["xf"] = AR[:, 32768:65536].bitcast(F32)
        b.dma(u3["cmask"], cmd, sem="dc")
        b.dma(u3["lnw1"], l1d, sem="dc")
        b.dma(u3["lnw2"], l2d, sem="dc")
        t_c3 = b.tok("dc")
        b.wait_only("scalar", [t_c3])
        b.wait_only("vector", [t_c3])
        emit_l1ffn(b, u3, {"cattn": cad, "cproj": cpd, "router": rtd1,
                           "w1": w1d1, "w2": w2d1}, xsp, xo,
                   phase_free=[], spilled=True, xtiles=xtiles,
                   sq_early=ARF[:, 30720:31744])
    return build_nc(bf)


def fused_maps(P, c):
    return {"x": None, "qT": P["l0_qT"], "kT": P["l0_kT"], "vT": P["l0_vT"],
            "oT": P["l0_oT"], "w1t": P["l0_w1t"], "w2t": P["l0_w2t"],
            "b1x": P["l0_b1"], "b2r": P["l0_b2"], "routerT": P["l0_routerT"],
            "delta": P["l1_delta"], "alphap": P["l1_alpha"],
            "betap": P["l1_beta"], "gammap": P["l1_gamma"],
            "omegab": P["l1_omega_b"], "tvalsc": P["tvals"],
            "bdiagc": P["bdiag"], "dftW": P["dftW"], "dftC": P["dftC"],
            "cattn": P["l1_cattn"], "cproj": P["l1_cproj"],
            "ln1b": P["l1_ln1_b"], "ln2b": P["l1_ln2_b"],
            "router": P["l1_router"], "w1": P["l1_w1"], "w2": P["l1_w2"],
            "identc": P["ident"], "onesc": P["ones"], "cmaskc": P["cmask"],
            "fconstc": P["fconst"]}


def p0_maps(P, c):
    return {"x": None, "qT": P["l0_qT"], "kT": P["l0_kT"],
            "vT": P["l0_vT"], "oT": P["l0_oT"], "identc": P["ident"],
            "onesc": P["ones"], "cmaskc": P["cmask"], "fconstc": P["fconst"]}


def _run_phase(build_fn, maps_fn, P, x):
    nc = build_fn(P)
    maps = []
    for c in range(NCORES):
        m = maps_fn(P, c)
        m["x"] = x[c]
        maps.append(m)
    res = run_bass_kernel_spmd(nc, maps, core_ids=list(range(NCORES)))
    return np.stack([r["xo"] for r in res.results])


FUSED = True


def kernel(**inputs):
    P = prep_inputs(inputs)
    x = P["x"]
    if FUSED:
        nc = build_fused(P)
        maps = []
        for c in range(NCORES):
            m = fused_maps(P, c)
            m["x"] = x[c]
            maps.append(m)
        res = run_bass_kernel_spmd(nc, maps, core_ids=list(range(NCORES)))
        return np.ascontiguousarray(np.stack([r["xo"] for r in res.results]))
    x = _run_phase(build_p0, p0_maps, P, x)
    x = _run_phase(build_p1, p1_maps, P, x)
    x = _run_phase(build_p2, p2_maps, P, x)
    x = _run_phase(build_p3, p3_maps, P, x)
    return np.ascontiguousarray(x)


# version 12
# speedup vs baseline: 1.0508x; 1.0508x over previous
import numpy as np
from contextlib import ExitStack

import concourse.bass as bass
import concourse.mybir as mybir
from concourse.bass_utils import run_bass_kernel_spmd

F32 = mybir.dt.float32
BF16 = mybir.dt.bfloat16
AF = mybir.ActivationFunctionType
ALU = mybir.AluOpType
AX = mybir.AxisListType

S = 2048
DM = 1024
NT = S // 128
NC = DM // 128
NCORES = 8
NFP = 2176
NFT = NFP // 128

ENGS = ("tensor", "scalar", "vector", "gpsimd", "sync")



class B:

    def __init__(self):
        self.ops = {e: [] for e in ENGS}
        self.cnt = {}

    def tok(self, name):
        return (name, self.cnt.get(name, 0))

    def emit(self, eng, fn, waits=(), inc=None, inc_amt=1):
        w = tuple((t[0], t[1]) for t in waits if t is not None and t[1] > 0)
        incs = []
        if inc is not None:
            for nm, amt in ([(inc, inc_amt)] if isinstance(inc, str) else inc):
                self.cnt[nm] = self.cnt.get(nm, 0) + amt
                incs.append((nm, amt))
        self.ops[eng].append((fn, w, tuple(incs)))
        return self.tok(incs[0][0]) if incs else None

    def dma(self, dst, src, waits=(), eng="gpsimd", sem="dg"):
        return self.emit(eng, lambda e, d=dst, s=src: e.dma_start(d, s),
                         waits=waits, inc=sem, inc_amt=16)

    def wait_only(self, eng, waits):
        self.ops[eng].append((lambda e: e.nop(), tuple(
            (t[0], t[1]) for t in waits if t is not None and t[1] > 0), ()))

    def join(self, waiters, producers=None):
        names = list(self.cnt) if producers is None else list(producers)
        toks = [self.tok(p) for p in names if self.cnt.get(p, 0) > 0]
        for eng in waiters:
            self.wait_only(eng, toks)


def _run_engine(nc, b, sems, name, eng):
    with nc.allow_low_precision(reason="bf16 matmul operands by design"):
        for fn, waits, incs in b.ops[name]:
            for (sn, v) in waits:
                eng.wait_ge(sems[sn], v)
            inst = fn(eng)
            for (sn, amt) in incs:
                inst.then_inc(sems[sn], amt)


def build_nc(build_fn):
    nc = bass.Bass("TRN2", target_bir_lowering=False, debug=False)
    b = B()
    with ExitStack() as stack:
        build_fn(nc, stack, b)
        sems = {}
        for name in b.cnt:
            sems[name] = stack.enter_context(nc.semaphore(name))
        block = stack.enter_context(nc.Block())
        for engname in ENGS:
            if b.ops[engname]:
                getattr(block, engname)(
                    lambda eng, en=engname: _run_engine(nc, b, sems, en, eng))
    return nc



def _dft_consts():
    N, L = 2 * S, S
    NF = N // 2 + 1
    t = np.arange(L, dtype=np.float64)
    f = np.arange(NFP, dtype=np.float64)
    ang = 2.0 * np.pi * np.outer(t, f) / N
    Wc = np.cos(ang)
    Ws = -np.sin(ang)
    Wc[:, NF:] = 0.0
    Ws[:, NF:] = 0.0
    alpha = np.full(NFP, 2.0)
    alpha[0] = 1.0
    alpha[NF - 1] = 1.0
    alpha[NF:] = 0.0
    ang2 = 2.0 * np.pi * np.outer(f, t) / N
    Cr = alpha[:, None] * np.cos(ang2) / N
    Ci = -alpha[:, None] * np.sin(ang2) / N
    Wf = np.stack([Wc, Ws]).reshape(2, 16, 128, NFT, 128)
    Wf = np.ascontiguousarray(Wf.transpose(0, 3, 1, 2, 4)).astype(np.float32)
    Cv = np.stack([Cr, Ci]).reshape(2, NFT, 128, 16, 128)
    Cv = np.ascontiguousarray(Cv.transpose(0, 1, 3, 2, 4)).astype(np.float32)
    return Wf, Cv


def _consts():
    c = {}
    c["ident"] = np.eye(128, dtype=np.float32)
    fc = np.zeros((128, 8), dtype=np.float32)
    fc[:, 0] = 1e-5
    fc[:, 1] = 1.0
    c["fconst"] = fc
    c["ones"] = np.ones((128, 128), dtype=np.float32)
    m = np.zeros((128, 4, 512), dtype=np.float32)
    for j in range(4):
        for p in range(128):
            m[p, j, p + j * 128:] = 1.0
    c["cmask"] = np.ascontiguousarray(m.reshape(128, 2048))
    c["tvals"] = np.broadcast_to(np.arange(S, dtype=np.float32), (128, S)).copy()
    bd = np.zeros((128, 16, 128), dtype=np.float32)
    for j in range(16):
        for p in range(128):
            bd[p, j, 8 * j + p // 16] = 1.0
    c["bdiag"] = np.ascontiguousarray(bd.reshape(128, 16 * 128))
    import ml_dtypes
    wf, cv = _dft_consts()
    c["dftW"] = np.ascontiguousarray(wf.transpose(1, 0, 2, 3, 4)).astype(
        ml_dtypes.bfloat16)
    c["dftC"] = np.ascontiguousarray(cv.transpose(1, 0, 2, 3, 4)).astype(
        ml_dtypes.bfloat16)
    return c


def _dn_layout(a):
    out = np.empty((128, 128), dtype=np.float32)
    d = np.arange(128) // 16
    n = np.arange(128) % 16
    for dt in range(128):
        out[:, dt] = a[dt * 8 + d, n]
    return np.ascontiguousarray(out)


def prep_inputs(inputs):
    f32 = lambda a: np.ascontiguousarray(np.asarray(a, dtype=np.float32))
    P = {}
    P["x"] = f32(inputs["L0_attn_x"])
    P["l0_qT"] = f32(np.asarray(inputs["L0_attn_q_proj"]).T)
    P["l0_kT"] = f32(np.asarray(inputs["L0_attn_k_proj"]).T)
    P["l0_vT"] = f32(np.asarray(inputs["L0_attn_v_proj"]).T)
    P["l0_oT"] = f32(np.asarray(inputs["L0_attn_o_proj"]).T)
    P["l0_routerT"] = f32(np.asarray(inputs["L0_ffn_router_w"]).T)
    P["l0_w1t"] = f32(np.asarray(inputs["L0_ffn_w1"]).transpose(0, 2, 1))
    P["l0_w2t"] = f32(np.asarray(inputs["L0_ffn_w2"]).transpose(0, 2, 1))
    b1 = np.asarray(inputs["L0_ffn_b1"])
    P["l0_b1"] = f32(b1.reshape(8, 32, 128).transpose(2, 0, 1).reshape(128, 256))
    P["l0_b2"] = f32(np.asarray(inputs["L0_ffn_b2"]).reshape(1, 8192))
    P["l1_delta"] = _dn_layout(np.asarray(inputs["L1_attn_delta"])[:, :, 0])
    P["l1_alpha"] = _dn_layout(np.asarray(inputs["L1_attn_alpha"])[:, :, 0])
    P["l1_beta"] = _dn_layout(np.asarray(inputs["L1_attn_beta"])[:, :, 0])
    P["l1_gamma"] = _dn_layout(np.asarray(inputs["L1_attn_gamma"]))
    P["l1_omega_b"] = f32(np.broadcast_to(np.asarray(inputs["L1_attn_omega"]), (128, 1024)))
    P["l1_ln1_b"] = f32(np.broadcast_to(np.asarray(inputs["L1_ffn_ln1_w"]), (128, 1024)))
    P["l1_ln2_b"] = f32(np.broadcast_to(np.asarray(inputs["L1_ffn_ln2_w"]), (128, 1024)))
    P["l1_cattn"] = f32(inputs["L1_ffn_c_attn_w"])
    P["l1_cproj"] = f32(inputs["L1_ffn_c_proj_w"])
    P["l1_router"] = f32(inputs["L1_ffn_router_w"])
    P["l1_w1"] = f32(inputs["L1_ffn_w1"])
    P["l1_w2"] = f32(inputs["L1_ffn_w2"])
    P.update(_consts())
    return P


def cmaj(dram_ap, r0, r1, c0, c1):
    return dram_ap[r0:r1, c0:c1].rearrange("(t p) c -> p t c", p=128)


def sb3(sbuf_ap, nt):
    return sbuf_ap.rearrange("p (t c) -> p t c", t=nt)



def emit_ln(b, x_ap, sq, st, make_dst, eps_ap, norm_free=None, wrow=None,
            x_ready=None, wrow_tok=()):
    inv = 1.0 / DM
    toks = []
    t_red_prev = None
    for ti in range(NT):
        o = (ti % 2) * 8
        xt = x_ap[:, ti * DM:(ti + 1) * DM]
        stat_free = [toks[ti - 2]] if ti >= 2 else []
        if x_ready:
            stat_free = stat_free + list(x_ready(ti))
        b.emit("vector", lambda e, xt=xt, o=o: e.reduce_sum(
            st[:, o + 0:o + 1], xt, axis=AX.X, negate=True),
            waits=stat_free, inc="dve")
        sq_waits = [t_red_prev] if t_red_prev else []
        if x_ready:
            sq_waits = sq_waits + list(x_ready(ti))
        t_sq = b.emit("scalar", lambda e, xt=xt: e.activation(
            sq[:], xt, AF.Square), waits=sq_waits, inc="act")
        t_red_prev = b.emit("vector", lambda e, o=o: e.reduce_sum(
            st[:, o + 1:o + 2], sq[:], axis=AX.X), waits=[t_sq], inc="dve")
        t_mean = b.emit("vector", lambda e, o=o: e.tensor_scalar_mul(
            st[:, o + 2:o + 3], st[:, o + 0:o + 1], inv),
            waits=[b.tok("dve")], inc="dve")
        t_m2 = b.emit("vector", lambda e, o=o: e.tensor_mul(
            st[:, o + 3:o + 4], st[:, o + 2:o + 3], st[:, o + 2:o + 3]),
            waits=[t_mean], inc="dve")
        t_var = b.emit("vector", lambda e, o=o: e.scalar_tensor_tensor(
            st[:, o + 4:o + 5], st[:, o + 1:o + 2], inv, st[:, o + 3:o + 4],
            ALU.mult, ALU.subtract), waits=[t_m2], inc="dve")
        t_std = b.emit("scalar", lambda e, o=o: e.activation(
            st[:, o + 5:o + 6], st[:, o + 4:o + 5], AF.Sqrt, bias=eps_ap),
            waits=[t_var], inc="act")
        t_rstd = b.emit("vector", lambda e, o=o: e.reciprocal(
            st[:, o + 6:o + 7], st[:, o + 5:o + 6]), waits=[t_std], inc="dve")
        t_nmr = b.emit("vector", lambda e, o=o: e.tensor_mul(
            st[:, o + 7:o + 8], st[:, o + 2:o + 3], st[:, o + 6:o + 7]),
            waits=[t_rstd], inc="dve")
        waits = [t_nmr] + (norm_free(ti) if norm_free else [])
        if wrow is None:
            t_norm = b.emit("scalar", lambda e, ti=ti, o=o, d=make_dst(ti):
                            e.activation(
                                d, x_ap[:, ti * DM:(ti + 1) * DM], AF.Identity,
                                scale=st[:, o + 6:o + 7], bias=st[:, o + 7:o + 8]),
                            waits=waits, inc="act")
        else:
            t_na = b.emit("scalar", lambda e, ti=ti, o=o: e.activation(
                sq[:], x_ap[:, ti * DM:(ti + 1) * DM], AF.Identity,
                scale=st[:, o + 6:o + 7], bias=st[:, o + 7:o + 8]),
                waits=waits + [t_red_prev], inc="act")
            t_norm = b.emit("vector", lambda e, ti=ti, d=make_dst(ti):
                            e.tensor_mul(d, sq[:], wrow),
                            waits=[t_na] + list(wrow_tok), inc="dve")
            t_red_prev = t_norm
        toks.append(t_norm)
    return toks


def emit_lnt_to_lnc(b, lnt, lnc, pt, ident, norm_toks, extra_wait=()):
    idx = 0
    ev = {}
    for ti in range(NT):
        for ci in range(NC):
            off = (idx % 2) * 1024 + ((idx // 2) % 4) * 128
            waits = [norm_toks[ti]] + list(extra_wait)
            if idx >= 2:
                waits.append(ev[idx - 2])
            if idx >= 8:
                waits.append(ev[idx - 8])
            t_tr = b.emit("tensor", lambda e, ti=ti, ci=ci, off=off: e.transpose(
                pt[:, off:off + 128],
                lnt[:, ti * DM + ci * 128:ti * DM + ci * 128 + 128], ident),
                waits=waits, inc="pe")
            ev[idx] = b.emit("scalar", lambda e, ti=ti, ci=ci, off=off: e.copy(
                lnc[:, ci * S + ti * 128:ci * S + ti * 128 + 128],
                pt[:, off:off + 128]),
                waits=[t_tr], inc="act")
            idx += 1
    ttok = [ev[(ti + 1) * NC - 1] for ti in range(NT)]
    return ev[idx - 1], b.tok("pe"), ttok



def emit_attention(b, u, lnc, dst, w_srcs, wo_src, lnc_tok, aoT_free,
                   dst_waits, wbuf_free, scratch_free, lnc_tt=None,
                   const_tok=None):
    qT, kT, vON, probsT, aoT = u["qT"], u["kT"], u["vON"], u["probsT"], u["aoT"]
    wqkv, wo = u["wqkv"], u["wo"]
    ones, cmask = u["ones"], u["cmask"]
    invr, tmp64 = u["invr"], u["tmp64"]
    ps, psy, ptf = u["ps"], u["psy"], u["ptf"]
    W = 1024
    NG = 8
    if lnc_tt is None:
        lnc_tt = [lnc_tok] * NT

    grp_tok = {}

    def issue_group_dma(g, waits):
        par = g % 2
        wb = par * 3 * W
        sem = "wg%d" % par
        wq_ap, wk_ap, wv_ap = w_srcs(g)
        b.dma(sb3(wqkv[:, wb:wb + W], NC), wq_ap, waits=waits, sem=sem)
        b.dma(sb3(wqkv[:, wb + W:wb + 2 * W], NC), wk_ap, sem=sem)
        grp_tok[g] = b.dma(sb3(wqkv[:, wb + 2 * W:wb + 3 * W], NC), wv_ap, sem=sem)

    issue_group_dma(0, list(wbuf_free))
    issue_group_dma(1, [])

    bank_ev = {0: None, 1: None, 2: None, 3: None}
    av_free = {0: None, 1: None}
    bc_ev = {0: None, 1: None}
    tmp_ev = {0: None, 1: None}
    last_n = None
    first_n = True
    grp_free = {}
    for g in range(NG):
        par = g % 2
        wb = par * 3 * W
        wtok = grp_tok[g]
        qkev = {}
        for wi, woff in ((0, 0), (1, W)):
            dstbuf = qT if wi == 0 else kT
            for tc in range(4):
                i = wi * 4 + tc
                bank = ps[i % 2]
                for ct in range(NC):
                    w0 = wb + woff + ct * 128
                    waits = ()
                    if ct == 0:
                        waits = [wtok, lnc_tt[tc * 4 + 3], bank_ev[i % 2]]
                        if g == 0 and i < 2:
                            waits = waits + list(scratch_free)
                    b.emit("tensor", lambda e, bank=bank, ct=ct, tc=tc, w0=w0:
                           e.matmul(bank[:, :], wqkv[:, w0:w0 + 128],
                                    lnc[:, ct * S + tc * 512:ct * S + tc * 512 + 512],
                                    start=(ct == 0), stop=(ct == NC - 1)),
                           waits=waits, inc=("pe" if ct == NC - 1 else None))
                t_mm = b.tok("pe")
                if i % 2 == 1:
                    t_ev = b.emit("scalar", lambda e, dstbuf=dstbuf, tc=tc:
                                  e.copy(dstbuf[:, (tc - 1) * 512:(tc + 1) * 512],
                                         u["psA"][:, :]),
                                  waits=[t_mm], inc="act")
                    bank_ev[0] = t_ev
                    bank_ev[1] = t_ev
                    qkev[i] = t_ev
        t_qk = qkev[7]
        t_z = b.emit("vector", lambda e: e.memset(vON[:, :], 0.0),
                     waits=([last_n] if last_n else list(scratch_free)),
                     inc="dve")
        t_vset = b.emit("vector", lambda e: e.memset(
            vON[:, 64:NT * 192:192], 1.0), waits=[t_z], inc="dve")
        t_vev = {}
        for tt in range(NT):
            bank = ps[2 + tt % 2]
            for ct in range(NC):
                waits = ()
                if ct == 0:
                    waits = [wtok, lnc_tt[tt], bank_ev[2 + tt % 2],
                             t_vev.get(tt - 2)]
                b.emit("tensor", lambda e, bank=bank, ct=ct, tt=tt, wb=wb:
                       e.matmul(bank[:, 0:128],
                                lnc[:, ct * S + tt * 128:ct * S + tt * 128 + 128],
                                wqkv[:, wb + 2 * W + ct * 128:wb + 2 * W + ct * 128 + 128],
                                start=(ct == 0), stop=(ct == NC - 1)),
                       waits=waits, inc=("pe" if ct == NC - 1 else None))
            t_mm = b.tok("pe")
            grp_free[g] = t_mm
            for hh in range(2):
                t_vev[tt] = b.emit("vector", lambda e, tt=tt, hh=hh, bank=bank:
                                   e.tensor_copy(
                                       vON[:, tt * 192 + hh * 128:
                                           tt * 192 + hh * 128 + 64],
                                       bank[:, hh * 64:hh * 64 + 64]),
                                   waits=([t_mm, t_vset] if hh == 0 else ()),
                                   inc="dve")
        t_v = t_vev[NT - 1]
        bank_ev[2] = t_v
        bank_ev[3] = t_v
        for qc in range(4):
            nkt = 4 * qc + 4
            av_bank = {0: psy[:, 0:512], 1: psy[:, 512:1024]}
            psum_pair_ap = {0: u["psA"][:, :], 1: u["psB"][:, :]}
            pslot = lambda hh, kt: (hh * 4 + kt % 4) * 512
            t_exq = {}
            av_pe = {}

            def emit_score(hh, kt, qc=qc):
                pbase = 64 * hh
                sbank = ps[2 * hh + kt % 2]
                t_sc = b.emit("tensor", lambda e, sbank=sbank, kt=kt, qc=qc,
                              pbase=pbase: e.matmul(
                                  sbank[:, :],
                                  kT[pbase:pbase + 64, kt * 128:kt * 128 + 128],
                                  qT[pbase:pbase + 64, qc * 512:qc * 512 + 512],
                                  start=True, stop=True),
                              waits=[t_qk, bank_ev[2 * hh + kt % 2]],
                              inc="pe")
                po = pslot(hh, kt)
                t_ex = b.emit("scalar", lambda e, sbank=sbank, po=po: e.activation(
                    probsT[:, po:po + 512], sbank[:, :], AF.Exp, scale=0.125),
                    waits=[t_sc, av_pe.get((hh, kt - 4))], inc="act")
                bank_ev[2 * hh + kt % 2] = t_ex
                jj = kt - 4 * qc
                if jj >= 0:
                    t_ex = b.emit("vector", lambda e, po=po, jj=jj: e.tensor_mul(
                        probsT[:, po:po + 512], probsT[:, po:po + 512],
                        cmask[:, jj * 512:jj * 512 + 512]),
                        waits=[t_ex] + ([const_tok] if const_tok else []),
                        inc="dve")
                t_exq[(hh, kt)] = t_ex

            def emit_av(hh, kt, qc=qc, nkt=nkt):
                po = pslot(hh, kt)
                waits_av = [t_exq[(hh, kt)]]
                if kt == 0:
                    waits_av += [av_free[hh], t_v]
                if hh == 0:
                    b.emit("tensor", lambda e, kt=kt, nkt=nkt, po=po: e.matmul(
                        av_bank[0][0:65, :],
                        vON[:, kt * 192:kt * 192 + 65],
                        probsT[:, po:po + 512],
                        start=(kt == 0), stop=(kt == nkt - 1)),
                        waits=waits_av, inc="pe")
                else:
                    b.emit("tensor", lambda e, kt=kt, nkt=nkt, po=po: e.matmul(
                        av_bank[1][0:128, :],
                        vON[:, kt * 192 + 64:kt * 192 + 192],
                        probsT[:, po:po + 512],
                        start=(kt == 0), stop=(kt == nkt - 1)),
                        waits=waits_av, inc="pe")
                av_pe[(hh, kt)] = b.tok("pe")

            for kt in range(nkt):
                emit_score(0, kt)
                emit_score(1, kt)
                if kt > 1:
                    emit_av(0, kt - 2)
                    emit_av(1, kt - 2)
            for k2 in (nkt - 2, nkt - 1):
                emit_av(0, k2)
                emit_av(1, k2)
            for hh in range(2):
                pbase = 64 * hh
                srow = 64 if hh == 0 else 0
                bcb = ptf[:, hh * 512:hh * 512 + 512]
                t_av = av_pe[(hh, nkt - 1)]
                t_inv = b.emit("vector", lambda e, hh=hh, srow=srow: e.reciprocal(
                    invr[srow:srow + 1, :], av_bank[hh][srow:srow + 1, :]),
                    waits=[t_av, bc_ev[hh]], inc="dve")
                t_bc = b.emit("tensor", lambda e, srow=srow, bcb=bcb: e.matmul(
                    bcb, ones[srow:srow + 1, 0:128],
                    invr[srow:srow + 1, :], start=True, stop=True),
                    waits=[t_inv, bc_ev[hh] if bc_ev[hh] else lnc_tok], inc="pe")
                t_tmp = b.emit("vector", lambda e, hh=hh, pbase=pbase:
                               e.tensor_copy(
                                   tmp64[pbase:pbase + 64, :],
                                   av_bank[hh][pbase:pbase + 64, :]),
                               waits=[t_av, tmp_ev[hh]], inc="dve")
                waits_n = [t_tmp, t_bc]
                if first_n:
                    waits_n += list(aoT_free)
                t_n = b.emit("vector", lambda e, hh=hh, g=g, qc=qc,
                             pbase=pbase, bcb=bcb: e.tensor_mul(
                                 aoT[pbase:pbase + 64,
                                     g * S + qc * 512:g * S + qc * 512 + 512],
                                 tmp64[pbase:pbase + 64, :],
                                 bcb[pbase:pbase + 64, :]),
                             waits=waits_n, inc="dve")
                first_n = False
                last_n = t_n
                bc_ev[hh] = t_n
                av_free[hh] = t_n
                tmp_ev[hh] = t_n
        if g + 2 < NG:
            issue_group_dma(g + 2, [grp_free[g]])
    t_ao = b.tok("dve")
    t_heads_pe = b.tok("pe")
    oev = {0: None, 1: None}
    for dh in range(2):
        t_wo = b.dma(sb3(wo[:, :], 8), wo_src(dh), sem="wo",
                     waits=(list(wbuf_free) if dh == 0 else [b.tok("pe")]))
        for tt in range(NT):
            bank = tt % 2
            for ft in range(8):
                waits = ()
                if ft == 0:
                    waits = [t_ao, t_wo, oev[bank], bank_ev[bank]]
                b.emit("tensor", lambda e, tt=tt, bank=bank, ft=ft: e.matmul(
                    ps[bank][:, :],
                    aoT[:, ft * S + tt * 128:ft * S + tt * 128 + 128],
                    wo[:, ft * 512:ft * 512 + 512],
                    start=(ft == 0), stop=(ft == 7)),
                    waits=waits, inc=("pe" if ft == 7 else None))
            t_mm = b.tok("pe")
            oev[bank] = b.emit("vector", lambda e, tt=tt, dh=dh, bank=bank: e.tensor_add(
                dst[:, tt * 1024 + dh * 512:tt * 1024 + dh * 512 + 512],
                dst[:, tt * 1024 + dh * 512:tt * 1024 + dh * 512 + 512],
                ps[bank][:, :]),
                waits=[t_mm] + list(dst_waits(tt)), inc="dve")
    return b.tok("dve"), t_heads_pe



def emit_moe(b, u, lnc, dst, E, F, fchunk, w1_src, w2_src, router_src,
             b1_col, b2_src, lnc_tok, dst_waits, scratch_free, wbuf_free,
             lnc_tt=None, wbuf2_free=None):
    ps, psy = u["ps"], u["psy"]
    h, w1b, w2b = u["h"], u["w1b"], u["w2b"]
    logits, mvec, sc8 = u["logits"], u["mvec"], u["sc8"]
    router_sb, ones = u["router_sb"], u["ones"]
    nq = F // fchunk
    nfs = fchunk // 128
    if lnc_tt is None:
        lnc_tt = [lnc_tok] * NT
    w1w = NC * fchunk
    w2w = nfs * 1024

    t_rt = b.dma(sb3(router_sb[:, :NC * E], NC), router_src, sem="dmr",
                 waits=list(scratch_free))
    rt_ev = {}
    for ts in range(NT):
        bank = ps[2 + ts % 2]
        for ct in range(NC):
            waits = ()
            if ct == 0:
                waits = [t_rt, lnc_tt[ts], rt_ev.get(ts - 2)] + (
                    list(scratch_free) if ts < 2 else [])
            b.emit("tensor", lambda e, bank=bank, ct=ct, ts=ts: e.matmul(
                bank[:, 0:E],
                lnc[:, ct * S + ts * 128:ct * S + ts * 128 + 128],
                router_sb[:, ct * E:ct * E + E],
                start=(ct == 0), stop=(ct == NC - 1)),
                waits=waits, inc=("pe" if ct == NC - 1 else None))
        t_mm = b.tok("pe")
        rt_ev[ts] = b.emit("scalar", lambda e, ts=ts, bank=bank: e.copy(
            logits[:, ts * E:ts * E + E], bank[:, 0:E]),
            waits=[t_mm], inc="act")
    mv_toks = []
    for ts in range(NT):
        lg = logits[:, ts * E:ts * E + E]
        t0 = b.emit("vector", lambda e, lg=lg: e.reduce_max(
            sc8[:, 8:9], lg, axis=AX.X), waits=[rt_ev[ts], b.tok("dve")],
            inc="dve")
        t1 = b.emit("vector", lambda e, lg=lg: e.tensor_scalar(
            sc8[:, 0:8], lg, sc8[:, 8:9], None, ALU.is_equal),
            waits=[t0], inc="dve")
        t2 = b.emit("vector", lambda e, lg=lg: e.scalar_tensor_tensor(
            sc8[:, 16:24], sc8[:, 0:8], -1e30, lg, ALU.mult, ALU.add),
            waits=[t1], inc="dve")
        t3 = b.emit("vector", lambda e: e.reduce_max(
            sc8[:, 9:10], sc8[:, 16:24], axis=AX.X), waits=[t2], inc="dve")
        t4 = b.emit("vector", lambda e: e.tensor_sub(
            sc8[:, 10:11], sc8[:, 9:10], sc8[:, 8:9]), waits=[t3], inc="dve")
        t5 = b.emit("scalar", lambda e: e.activation(
            sc8[:, 11:12], sc8[:, 10:11], AF.Sigmoid), waits=[t4], inc="act")
        t6 = b.emit("vector", lambda e: e.tensor_scalar(
            sc8[:, 12:13], sc8[:, 11:12], -1.0, 1.0, ALU.mult, ALU.add),
            waits=[t5], inc="dve")
        t7 = b.emit("vector", lambda e, lg=lg: e.tensor_scalar(
            sc8[:, 16:24], lg, sc8[:, 9:10], None, ALU.is_equal),
            waits=[t6], inc="dve")
        t8 = b.emit("vector", lambda e: e.tensor_scalar_mul(
            sc8[:, 16:24], sc8[:, 16:24], sc8[:, 11:12]), waits=[t7], inc="dve")
        t9 = b.emit("vector", lambda e, ts=ts: e.scalar_tensor_tensor(
            mvec[:, ts * E:ts * E + E], sc8[:, 0:8], sc8[:, 12:13],
            sc8[:, 16:24], ALU.mult, ALU.add), waits=[t8], inc="dve")
        mv_toks.append(t9)
    t_mv = mv_toks[-1]

    mvT_tok = {}
    t_b2all = None
    if b2_src is not None:
        mvT = u["lnt"][0:8, 6400:6400 + NT * 128]
        mvc = u["lnt"][:, 6144:6144 + NT * 8]
        t_b2all = b.dma(u["b2r"][0:8, :], b2_src, waits=list(scratch_free),
                        sem="db2")
        tr_ev = {}
        for ts in range(NT):
            slot = ts % 2
            t_cst = b.emit("vector", lambda e, ts=ts: e.tensor_copy(
                mvc[:, ts * 8:ts * 8 + 8], mvec[:, ts * 8:ts * 8 + 8]),
                waits=[mv_toks[ts]], inc="dve")
            t_tr = b.emit("tensor", lambda e, ts=ts, slot=slot: e.transpose(
                u["pt"][0:8, slot * 1024:slot * 1024 + 128],
                mvc[:, ts * 8:ts * 8 + 8], u["ident"][:, :]),
                waits=[t_cst, tr_ev.get(ts - 2)] + (list(scratch_free)
                                                    if ts < 2 else []),
                inc="pe")
            tr_ev[ts] = b.emit("scalar", lambda e, ts=ts, slot=slot: e.copy(
                mvT[0:8, ts * 128:ts * 128 + 128],
                u["pt"][0:8, slot * 1024:slot * 1024 + 128]),
                waits=[t_tr], inc="act")
            mvT_tok[ts] = tr_ev[ts]

    nchunks = E * nq
    w_tok = {}

    def issue_w1_dma(i, waits):
        par = i % 2
        e_, q_ = i // nq, i % nq
        b.dma(sb3(w1b[:, par * w1w:par * w1w + w1w], NC), w1_src(e_, q_),
              waits=waits, sem="wf%d" % par)

    def issue_w2_dma(i, waits):
        par = i % 2
        e_, q_ = i // nq, i % nq
        w_tok[i] = b.dma(sb3(w2b[:, par * w2w:par * w2w + w2w], nfs),
                         w2_src(e_, q_), waits=waits, sem="wf%d" % par)

    issue_w1_dma(0, list(wbuf_free))
    issue_w2_dma(0, list(wbuf2_free) if wbuf2_free is not None else [])
    issue_w1_dma(1, [])
    issue_w2_dma(1, [])
    h_ev = {}
    y_ev = {0: None, 1: None}
    chunk_free = {}
    for i in range(nchunks):
        par = i % 2
        e_, q_ = i // nq, i % nq
        wtok = w_tok[i]
        last_q = (q_ == nq - 1)
        for tc in range(4):
            gl_toks = []
            for fs in range(nfs):
                bank = ps[fs % 2]
                for ct in range(NC):
                    waits = ()
                    if ct == 0:
                        waits = [wtok, lnc_tt[tc * 4 + 3], h_ev.get(fs % 2)]
                    b.emit("tensor", lambda e, bank=bank, ct=ct, tc=tc, fs=fs,
                           par=par: e.matmul(
                               bank[:, :],
                               w1b[:, par * w1w + ct * fchunk + fs * 128:
                                   par * w1w + ct * fchunk + fs * 128 + 128],
                               lnc[:, ct * S + tc * 512:ct * S + tc * 512 + 512],
                               start=(ct == 0), stop=(ct == NC - 1)),
                           waits=waits, inc=("pe" if ct == NC - 1 else None))
                t_mm = b.tok("pe")
                bias = b1_col(e_, q_, fs) if b1_col else 0.0
                h_ev[fs % 2] = b.emit("scalar", lambda e, fs=fs, bank=bank,
                                      bias=bias: e.activation(
                                          h[:, fs * 512:fs * 512 + 512],
                                          bank[:, :], AF.Gelu, bias=bias),
                                      waits=[t_mm], inc="act")
                gl_toks.append(h_ev[fs % 2])
            t_h = gl_toks[-1]
            if tc == 3 and i + 2 < nchunks:
                issue_w1_dma(i + 2, [b.tok("pe")])
            for ts in range(4):
                ta = tc * 4 + ts
                slot = ta % 2
                if slot == 0:
                    ybanks = [(psy, 0), (psy, 512)]
                else:
                    ybanks = [(ps[2], 0), (ps[3], 0)]
                nmm = nfs
                for dh in range(2):
                    yb, yo = ybanks[dh]
                    for fs in range(nfs):
                        waits = ()
                        if fs == 0:
                            waits = [t_h, y_ev[slot]]
                        b.emit("tensor", lambda e, yb=yb, yo=yo, fs=fs, ts=ts,
                               dh=dh, par=par: e.matmul(
                                   yb[:, yo:yo + 512],
                                   h[:, fs * 512 + ts * 128:fs * 512 + ts * 128 + 128],
                                   w2b[:, par * w2w + fs * 1024 + dh * 512:
                                       par * w2w + fs * 1024 + dh * 512 + 512],
                                   start=(fs == 0), stop=(fs == nfs - 1)),
                               waits=waits, inc=("pe" if fs == nfs - 1
                                                 else None))
                t_y = b.tok("pe")
                evs = []
                for dh in range(2):
                    yb, yo = ybanks[dh]
                    evs.append(b.emit("vector", lambda e, yb=yb, yo=yo, ta=ta,
                                      dh=dh, e_=e_: e.scalar_tensor_tensor(
                                          dst[:, ta * 1024 + dh * 512:
                                              ta * 1024 + dh * 512 + 512],
                                          yb[:, yo:yo + 512],
                                          mvec[:, ta * E + e_:ta * E + e_ + 1],
                                          dst[:, ta * 1024 + dh * 512:
                                              ta * 1024 + dh * 512 + 512],
                                          ALU.mult, ALU.add),
                                      waits=[t_y, t_mv] + list(dst_waits(ta)),
                                      inc="dve"))
                y_ev[slot] = evs[-1]
        chunk_free[i] = b.tok("pe")
        if i + 2 < nchunks:
            issue_w2_dma(i + 2, [chunk_free[i]])
    if b2_src is not None:
        t_de = b.tok("dve")
        bias_ev = {0: None, 1: None}
        for ta in range(NT):
            for dh in range(2):
                bank = ps[dh]
                t_bm = b.emit("tensor", lambda e, ta=ta, dh=dh, bank=bank:
                              e.matmul(bank[:, :],
                                       mvT[0:8, ta * 128:ta * 128 + 128],
                                       u["b2r"][0:8, dh * 512:dh * 512 + 512],
                                       start=True, stop=True),
                              waits=[t_b2all, mvT_tok[ta], h_ev.get(dh),
                                     bias_ev[dh]], inc="pe")
                bias_ev[dh] = b.emit("vector", lambda e, ta=ta, dh=dh,
                                     bank=bank: e.tensor_add(
                                         dst[:, ta * 1024 + dh * 512:
                                             ta * 1024 + dh * 512 + 512],
                                         dst[:, ta * 1024 + dh * 512:
                                             ta * 1024 + dh * 512 + 512],
                                         bank[:, :]),
                                     waits=[t_bm, t_de if ta == 0 else None],
                                     inc="dve")
    return b.tok("dve")


def alloc_moe(nc, stack, u, E=8, maxf=1024, weights=True):
    sb = lambda name, shape, dt: stack.enter_context(
        nc.sbuf_tensor("sb_" + name, shape, dt))
    if weights:
        u["w1b"] = sb("w1b", [128, 2 * NC * maxf], BF16)
        u["w2b"] = sb("w2b", [128, 2 * (maxf // 128) * 1024], BF16)
    u["logits"] = sb("logits", [128, NT * E], F32)
    u["mvec"] = sb("mvec", [128, NT * E], F32)
    u["sc8"] = sb("sc8", [128, 24], F32)
    u["router_sb"] = sb("router_sb", [128, NC * E], BF16)



def emit_conv(b, u, dr, norm_toks, scratch_free):
    ps, psy, pt = u["ps"], u["psy"], u["pt"]
    x, lnt = u["x"], u["lnt"]
    prm, tvals, bdiag = u["prm"], u["tvals"], u["bdiag"]
    e2f, e2m, ksb, kt = u["e2f"], u["e2m"], u["ksb"], u["kt"]
    zr, zi, wstr, pw, omgb = u["zr"], u["zi"], u["wstr"], u["pw"], u["omgb"]
    ident = u["ident"][:, :]

    C = lambda j: prm[:, j * 128:(j + 1) * 128]
    t_pl = b.dma(prm[:, 0:128], dr["delta"], sem="dpr", waits=list(scratch_free))
    b.dma(prm[:, 128:256], dr["alpha"], sem="dpr")
    b.dma(prm[:, 256:384], dr["beta"], sem="dpr")
    t_pl = b.dma(prm[:, 384:512], dr["gamma"], sem="dpr")
    t1 = b.emit("scalar", lambda e: e.activation(C(4), C(0), AF.Sigmoid),
                waits=[t_pl], inc="act")
    t2 = b.emit("scalar", lambda e: e.activation(C(5), C(1), AF.Sigmoid),
                waits=[t_pl], inc="act")
    t3 = b.emit("vector", lambda e: e.tensor_mul(C(5), C(4), C(5)),
                waits=[t2, b.tok("dve")], inc="dve")
    t4 = b.emit("vector", lambda e: e.tensor_scalar(C(5), C(5), -1.0, 1.0,
                                                    ALU.mult, ALU.add),
                waits=[t3], inc="dve")
    t_lq = b.emit("scalar", lambda e: e.activation(C(5), C(5), AF.Ln),
                  waits=[t4], inc="act")
    t5 = b.emit("vector", lambda e: e.tensor_mul(C(6), C(4), C(2)),
                waits=[t1, t_pl], inc="dve")
    t6 = b.emit("vector", lambda e: e.tensor_mul(C(6), C(6), C(3)),
                waits=[t5], inc="dve")
    t_c = b.emit("vector", lambda e: e.tensor_scalar_mul(C(6), C(6), 0.25),
                 waits=[t6], inc="dve")

    kev = {}
    e2_pe = {}
    t_exp = {}
    t_mul = {}
    for dt in range(128):
        db, j = dt // 16, dt % 16
        par = dt % 2
        waits = [t_lq] if dt == 0 else [t_mul[dt - 1]]
        waits = waits + [e2_pe.get(dt - 2)]
        t_exp[dt] = b.emit("scalar", lambda e, dt=dt, par=par: e.activation(
            e2m[:, par * 2048:par * 2048 + 2048], tvals[:, :], AF.Exp,
            scale=prm[:, 5 * 128 + dt:5 * 128 + dt + 1]),
            waits=waits, inc="act")
        t_mul[dt] = b.emit("vector", lambda e, dt=dt, par=par: e.tensor_scalar_mul(
            e2m[:, par * 2048:par * 2048 + 2048],
            e2m[:, par * 2048:par * 2048 + 2048],
            prm[:, 6 * 128 + dt:6 * 128 + dt + 1]),
            waits=[t_exp[dt], t_c], inc="dve")
        for lc in range(4):
            inc = None
            if lc == 3:
                inc = [("pe", 1)]
            b.emit("tensor", lambda e, dt=dt, j=j, lc=lc, par=par: e.matmul(
                ps[lc][:, :], bdiag[:, j * 128:(j + 1) * 128],
                e2m[:, par * 2048 + lc * 512:par * 2048 + lc * 512 + 512],
                start=(j == 0), stop=(j == 15)),
                waits=([t_mul[dt], kev.get(db - 1)] if lc == 0 else ()),
                inc=inc)
        e2_pe[dt] = b.tok("pe")
        if j == 15:
            t_mm = b.tok("pe")
            for lc in range(4):
                kev[db] = b.emit("scalar", lambda e, db=db, lc=lc: e.copy(
                    ksb[:, db * 2048 + lc * 512:db * 2048 + lc * 512 + 512],
                    ps[lc][:, :]), waits=[t_mm] if lc == 0 else (), inc="act")
    idx = 0
    tev = {}
    for db in range(8):
        for lt in range(16):
            off = (idx % 2) * 1024 + ((idx // 2) % 4) * 128
            waits = [kev[db]]
            if idx >= 2:
                waits.append(tev[idx - 2])
            if idx >= 8:
                waits.append(tev[idx - 8])
            t_tr = b.emit("tensor", lambda e, db=db, lt=lt, off=off: e.transpose(
                pt[:, off:off + 128],
                ksb[:, db * 2048 + lt * 128:db * 2048 + lt * 128 + 128], ident),
                waits=waits, inc="pe")
            tev[idx] = b.emit("scalar", lambda e, db=db, lt=lt, off=off: e.copy(
                kt[:, lt * 1024 + db * 128:lt * 1024 + db * 128 + 128],
                pt[:, off:off + 128]), waits=[t_tr], inc="act")
            idx += 1
    t_kt = tev[idx - 1]

    t_og = b.dma(omgb, dr["omegab"], waits=[b.tok("pe")], sem="dog")

    inv_bank_ev = {}
    inv_free = []
    xtile_toks = [None] * NT
    for dh in range(2):
        str_tok = {}
        str_free = {}

        def issue_strip(ft, waits, dh=dh):
            par = ft % 2
            str_tok[ft] = b.dma(
                sb3(wstr[:, par * 4096:par * 4096 + 4096], 32),
                dr["dftW"][ft].rearrange("cs lt p f -> p (cs lt) f"),
                waits=waits, sem="dw%d" % par)

        issue_strip(0, [b.tok("pe")] + (list(scratch_free) if dh == 0 else []))
        issue_strip(1, [])
        pw_ev = None
        z_toks = []
        evt_last = None
        for ft in range(NFT):
            par = ft % 2
            wb = par * 4096
            for lt in range(16):
                for ci in range(2):
                    for xk in range(2):
                        bank = ps[ci * 2 + xk]
                        rhs = (lnt if xk == 0 else kt)
                        waits = ()
                        if lt == 0:
                            waits = [str_tok[ft], t_kt]
                            if evt_last is not None:
                                waits.append(evt_last)
                            else:
                                waits += inv_free
                            if dh == 0 and ft == 0:
                                waits = list(waits) + [norm_toks[-1]]
                        b.emit("tensor", lambda e, bank=bank, ci=ci, lt=lt,
                               wb=wb, rhs=rhs, dh=dh: e.matmul(
                                   bank[:, :],
                                   wstr[:, wb + ci * 2048 + lt * 128:
                                        wb + ci * 2048 + lt * 128 + 128],
                                   rhs[:, lt * 1024 + dh * 512:
                                       lt * 1024 + dh * 512 + 512],
                                   start=(lt == 0), stop=(lt == 15)),
                               waits=waits,
                               inc=("pe" if lt == 15 and ci == 1 and xk == 1
                                    else None))
            t_f = b.tok("pe")
            if ft + 2 < NFT:
                issue_strip(ft + 2, [t_f])
            evt = []
            for q in range(4):
                w_ev = ()
                if q == 0:
                    w_ev = [t_f] + ([pw_ev] if pw_ev else [])
                evt.append(b.emit("scalar", lambda e, q=q: e.copy(
                    pw[:, q * 512:q * 512 + 512], ps[q][:, :]),
                    waits=w_ev, inc="act"))
            evt_last = evt[-1]
            P0 = lambda q: pw[:, q * 512:(q + 1) * 512]
            w0 = [evt[-1], pw_ev] if pw_ev else [evt[-1]]
            a1 = b.emit("vector", lambda e: e.tensor_mul(
                pw[:, 2048:2560], P0(0), P0(1)), waits=w0 + [b.tok("dve")],
                inc="dve")
            a2 = b.emit("vector", lambda e: e.tensor_mul(
                pw[:, 2560:3072], P0(2), P0(3)), waits=[a1], inc="dve")
            a3 = b.emit("vector", lambda e, ft=ft: e.tensor_sub(
                zr[:, ft * 512:ft * 512 + 512],
                pw[:, 2048:2560], pw[:, 2560:3072]), waits=[a2], inc="dve")
            a4 = b.emit("vector", lambda e: e.tensor_mul(
                pw[:, 2048:2560], P0(0), P0(3)), waits=[a3], inc="dve")
            a5 = b.emit("vector", lambda e: e.tensor_mul(
                pw[:, 2560:3072], P0(2), P0(1)), waits=[a4], inc="dve")
            a6 = b.emit("vector", lambda e, ft=ft: e.tensor_add(
                zi[:, ft * 512:ft * 512 + 512],
                pw[:, 2048:2560], pw[:, 2560:3072]), waits=[a5], inc="dve")
            pw_ev = a6
            z_toks.append(a6)
        t_z = z_toks[-1]
        cs_tok = {}

        def issue_cstrip(ft, waits, dh=dh):
            par = ft % 2
            cs_tok[ft] = b.dma(
                sb3(wstr[:, par * 4096:par * 4096 + 4096], 32),
                dr["dftC"][ft].rearrange("cs lt p f -> p (cs lt) f"),
                waits=waits, sem="dwc%d" % par)
        ptf = u["ptf"]
        ibanks = [psy[:, 0:512], psy[:, 512:1024], ps[0][:, :], ps[1][:, :],
                  ps[2][:, :], ps[3][:, :], ptf[:, 0:512], ptf[:, 512:1024]]
        lgs = [(0, 8), (8, 8)]
        for lg, (lbase, lcnt) in enumerate(lgs):
            issue_cstrip(0, [b.tok("pe")])
            issue_cstrip(1, [])
            for ft in range(NFT):
                for li in range(lcnt):
                    lt = lbase + li
                    bank = ibanks[li]
                    for ci in range(2):
                        zb = zr if ci == 0 else zi
                        waits = ()
                        if ft == 0 and ci == 0:
                            waits = [t_z, cs_tok[0], inv_bank_ev.get(li)]
                        elif li == 0 and ci == 0:
                            waits = [cs_tok[ft]]
                        b.emit("tensor", lambda e, bank=bank, lt=lt, ft=ft,
                               ci=ci, zb=zb: e.matmul(
                                   bank,
                                   wstr[:, (ft % 2) * 4096 + ci * 2048 +
                                        (lt % 16) * 128:
                                        (ft % 2) * 4096 + ci * 2048 +
                                        (lt % 16) * 128 + 128],
                                   zb[:, ft * 512:ft * 512 + 512],
                                   start=(ft == 0 and ci == 0),
                                   stop=(ft == NFT - 1 and ci == 1)),
                               waits=waits,
                               inc=("pe" if ci == 1 and li == lcnt - 1
                                    else None))
                if ft + 2 < NFT:
                    issue_cstrip(ft + 2, [b.tok("pe")])
            t_inv = b.tok("pe")
            r_last = {}
            for li in range(lcnt):
                lt = lbase + li
                so = (li % 2) * 512
                bank = ibanks[li]
                xs = x[:, lt * 1024 + dh * 512:lt * 1024 + dh * 512 + 512]
                r1 = b.emit("vector", lambda e, lt=lt, dh=dh, so=so: e.tensor_mul(
                    u["sq"][:, so:so + 512],
                    lnt[:, lt * 1024 + dh * 512:lt * 1024 + dh * 512 + 512],
                    omgb[:, dh * 512:dh * 512 + 512]),
                    waits=[t_inv if li < 2 else None, r_last.get(li - 2),
                           t_og], inc="dve")
                r2 = b.emit("vector", lambda e, xs=xs, bank=bank: e.tensor_add(
                    xs, xs, bank), waits=[r1, norm_toks[lt]], inc="dve")
                r_last[li] = b.emit("vector", lambda e, xs=xs, so=so: e.tensor_add(
                    xs, xs, u["sq"][:, so:so + 512]), waits=[r2], inc="dve")
                inv_bank_ev[li] = r2
                if dh == 1:
                    xtile_toks[lt] = r_last[li]
            inv_free = [r_last[lcnt - 2], r_last[lcnt - 1]]
    return b.tok("dve"), xtile_toks


def alloc_conv(nc, stack, u):
    sbl = lambda name, shape, dt: stack.enter_context(
        nc.sbuf_tensor("sb_" + name, shape, dt))
    u["prm"] = sbl("prm", [128, 8 * 128], F32)
    u["bdiag"] = sbl("bdiag", [128, 2048], BF16)
    big1 = sbl("big1", [128, 2 * NFT * 512], BF16)
    u["ksb"] = big1[:, 0:16384]
    u["zr"] = big1[:, 0:NFT * 512]
    u["zi"] = big1[:, NFT * 512:2 * NFT * 512]
    big2 = sbl("big2", [128, 8192], BF16)
    u["e2f"] = big2.bitcast(F32)[:, 0:2048]
    u["e2m"] = big2[:, 4096:8192]
    u["wstr"] = big2[:, 0:8192]
    big3 = sbl("big3", [128, 6 * 512], F32)
    u["tvals"] = big3[:, 0:2048]
    u["pw"] = big3[:, 0:3072]
    u["kt"] = sbl("kt", [128, 16384], BF16)
    u["omgb"] = sbl("omgb", [128, 1024], F32)[:, :]


def build_p2(P):
    def bf(nc, stack, b):
        xd = nc.dram_tensor("x", [S, DM], F32, kind="ExternalInput").ap()
        dd = nc.dram_tensor("delta", [128, 128], F32, kind="ExternalInput").ap()
        ad = nc.dram_tensor("alphap", [128, 128], F32, kind="ExternalInput").ap()
        bd = nc.dram_tensor("betap", [128, 128], F32, kind="ExternalInput").ap()
        gd = nc.dram_tensor("gammap", [128, 128], F32, kind="ExternalInput").ap()
        od = nc.dram_tensor("omegab", [128, 1024], F32, kind="ExternalInput").ap()
        td = nc.dram_tensor("tvalsc", [128, 2048], F32, kind="ExternalInput").ap()
        bdd = nc.dram_tensor("bdiagc", [128, 2048], F32, kind="ExternalInput").ap()
        wfd = nc.dram_tensor("dftW", [NFT, 2, 16, 128, 128], BF16,
                             kind="ExternalInput").ap()
        cvd = nc.dram_tensor("dftC", [NFT, 2, 16, 128, 128], BF16,
                             kind="ExternalInput").ap()
        idd = nc.dram_tensor("identc", [128, 128], F32, kind="ExternalInput").ap()
        fcd = nc.dram_tensor("fconstc", [128, 8], F32, kind="ExternalInput").ap()
        xo = nc.dram_tensor("xo", [S, DM], F32, kind="ExternalOutput").ap()

        u = alloc_common(nc, stack)
        sbl = lambda name, shape, dt: stack.enter_context(
            nc.sbuf_tensor("sb_" + name, shape, dt))
        u["lnt"] = sbl("lnt", [128, NT * DM], BF16)
        alloc_conv(nc, stack, u)

        b.dma(u["ident"][:, :], idd, sem="dc")
        b.dma(u["fc"][:, :], fcd, sem="dc")
        b.dma(u["tvals"][:, :], td, sem="dc")
        b.dma(u["bdiag"][:, :], bdd, sem="dc")
        t_c = b.tok("dc")
        t_x = b.dma(sb3(u["x"][:, :], NT), xd.rearrange("(t p) d -> p t d", p=128),
                    eng="sync", sem="ds")
        b.wait_only("vector", [t_x, t_c])
        b.wait_only("scalar", [t_x, t_c])
        b.wait_only("tensor", [t_c])
        norm_toks = emit_ln(b, u["x"][:, :], u["sq"], u["st"],
                            lambda ti: u["lnt"][:, ti * DM:(ti + 1) * DM],
                            u["fc"][:, 0:1])
        t_all, _ = emit_conv(b, u, {"delta": dd, "alpha": ad, "beta": bd,
                                    "gamma": gd, "dftW": wfd, "dftC": cvd,
                                    "omegab": od},
                             norm_toks, scratch_free=[])
        t_st = b.dma(xo.rearrange("(t p) d -> p t d", p=128), sb3(u["x"][:, :], NT),
                     waits=[t_all], eng="sync", sem="ds")
        b.wait_only("sync", [t_st])
    return build_nc(bf)


def p2_maps(P, c):
    return {"x": None, "delta": P["l1_delta"], "alphap": P["l1_alpha"],
            "betap": P["l1_beta"], "gammap": P["l1_gamma"],
            "omegab": P["l1_omega_b"], "tvalsc": P["tvals"],
            "bdiagc": P["bdiag"], "dftW": P["dftW"], "dftC": P["dftC"],
            "identc": P["ident"], "fconstc": P["fconst"]}



def alloc_common(nc, stack, with_x=True):
    sb = lambda name, shape, dt: stack.enter_context(
        nc.sbuf_tensor("sb_" + name, shape, dt))
    u = {}
    if with_x:
        u["x"] = sb("x", [128, NT * DM], F32)
    u["sq"] = sb("sq", [128, DM], F32)
    u["st"] = sb("st", [128, 16], F32)
    u["ident"] = sb("ident", [128, 128], BF16)
    u["ones"] = sb("ones", [128, 128], BF16)
    u["fc"] = sb("fc", [128, 8], F32)
    u["psA"] = stack.enter_context(nc.psum_tensor("psA", [128, 1024], F32))
    u["psB"] = stack.enter_context(nc.psum_tensor("psB", [128, 1024], F32))
    u["ps"] = [u["psA"][:, 0:512], u["psA"][:, 512:1024],
               u["psB"][:, 0:512], u["psB"][:, 512:1024]]
    u["psy"] = stack.enter_context(nc.psum_tensor("psy", [128, 1024], F32))
    u["ptb"] = stack.enter_context(nc.psum_tensor("ptb", [128, 2048], BF16))
    u["pt"] = u["ptb"]
    u["ptf"] = u["ptb"].bitcast(F32)
    return u


def load_common_consts(b, u, dr):
    b.dma(u["ident"][:, :], dr["ident"])
    b.dma(u["ones"][:, :], dr["ones"])


def alloc_attn(nc, stack, u):
    sb = lambda name, shape, dt: stack.enter_context(
        nc.sbuf_tensor("sb_" + name, shape, dt))
    u["lnt"] = sb("lnt", [128, NT * DM], BF16)
    u["lnc"] = sb("lnc", [128, NC * S], BF16)
    u["qT"] = sb("qT", [128, S], BF16)
    u["kT"] = sb("kT", [128, S], BF16)
    u["vON"] = sb("vON", [128, NT * 192], BF16)
    u["probsT"] = sb("probsT", [128, 16 * 512], BF16)
    u["aoT"] = u["lnt"]
    u["wqkv"] = sb("wqkv", [128, 2 * 3 * 1024], BF16)
    u["wo"] = sb("wo", [128, 8 * 512], BF16)
    u["invr"] = sb("invr", [128, 512], BF16)
    u["tmp64"] = sb("tmp64", [128, 512], BF16)
    u["cmask"] = sb("cmask", [128, 2048], BF16)


def emit_attn_block(b, u, dr_w, wbuf_free=(), x_free=(), scratch_free=()):
    norm_toks = emit_ln(b, u["x"][:, :], u["sq"], u["st"],
                        lambda ti: u["lnt"][:, ti * DM:(ti + 1) * DM],
                        u["fc"][:, 0:1], norm_free=lambda ti: list(x_free))
    lnc_tok, tp_tok, lnc_tt = emit_lnt_to_lnc(b, u["lnt"], u["lnc"], u["pt"],
                                              u["ident"][:, :], norm_toks)
    w_srcs = lambda g: (cmaj(dr_w["q"], 0, DM, g * 128, (g + 1) * 128),
                        cmaj(dr_w["k"], 0, DM, g * 128, (g + 1) * 128),
                        cmaj(dr_w["v"], 0, DM, g * 128, (g + 1) * 128))
    wo_src = lambda dh: cmaj(dr_w["o"], 0, DM, dh * 512, (dh + 1) * 512)
    return emit_attention(b, u, u["lnc"], u["x"], w_srcs, wo_src,
                          lnc_tok, [tp_tok], lambda tt: [norm_toks[tt]],
                          wbuf_free, list(scratch_free), lnc_tt=lnc_tt)[0]



def build_p0(P):
    def bf(nc, stack, b):
        xd = nc.dram_tensor("x", [S, DM], F32, kind="ExternalInput").ap()
        qd = nc.dram_tensor("qT", [DM, DM], F32, kind="ExternalInput").ap()
        kd = nc.dram_tensor("kT", [DM, DM], F32, kind="ExternalInput").ap()
        vd = nc.dram_tensor("vT", [DM, DM], F32, kind="ExternalInput").ap()
        od = nc.dram_tensor("oT", [DM, DM], F32, kind="ExternalInput").ap()
        idd = nc.dram_tensor("identc", [128, 128], F32, kind="ExternalInput").ap()
        oned = nc.dram_tensor("onesc", [128, 128], F32, kind="ExternalInput").ap()
        cmd = nc.dram_tensor("cmaskc", [128, 2048], F32, kind="ExternalInput").ap()
        xo = nc.dram_tensor("xo", [S, DM], F32, kind="ExternalOutput").ap()

        u = alloc_common(nc, stack)
        alloc_attn(nc, stack, u)
        fcd = nc.dram_tensor("fconstc", [128, 8], F32, kind="ExternalInput").ap()
        b.dma(u["ident"][:, :], idd, sem="dc")
        b.dma(u["ones"][:, :], oned, sem="dc")
        b.dma(u["cmask"][:, :], cmd, sem="dc")
        b.dma(u["fc"][:, :], fcd, sem="dc")
        t_c = b.tok("dc")
        t_x = b.dma(sb3(u["x"][:, :], NT), xd.rearrange("(t p) d -> p t d", p=128),
                    eng="sync", sem="ds")
        b.wait_only("vector", [t_x, t_c])
        b.wait_only("scalar", [t_x, t_c])
        b.wait_only("tensor", [t_c])
        t_all = emit_attn_block(b, u, {"q": qd, "k": kd, "v": vd, "o": od})
        t_st = b.dma(xo.rearrange("(t p) d -> p t d", p=128), sb3(u["x"][:, :], NT),
                     waits=[t_all], eng="sync", sem="ds")
        b.wait_only("sync", [t_st])
    return build_nc(bf)


def emit_ln_lnc(b, u, x_free=(), wrow=None):
    norm_toks = emit_ln(b, u["x"][:, :], u["sq"], u["st"],
                        lambda ti: u["lnt"][:, ti * DM:(ti + 1) * DM],
                        u["fc"][:, 0:1], norm_free=lambda ti: list(x_free))
    lnc_tok, tp_tok, lnc_tt = emit_lnt_to_lnc(b, u["lnt"], u["lnc"], u["pt"],
                                              u["ident"][:, :], norm_toks)
    return norm_toks, lnc_tok, tp_tok, lnc_tt


def build_p1(P):
    def bf(nc, stack, b):
        xd = nc.dram_tensor("x", [S, DM], F32, kind="ExternalInput").ap()
        w1d = nc.dram_tensor("w1t", [8, DM, 4096], F32, kind="ExternalInput").ap()
        w2d = nc.dram_tensor("w2t", [8, 4096, DM], F32, kind="ExternalInput").ap()
        b1d = nc.dram_tensor("b1x", [128, 256], F32, kind="ExternalInput").ap()
        b2d = nc.dram_tensor("b2r", [1, 8192], F32, kind="ExternalInput").ap()
        rtd = nc.dram_tensor("routerT", [DM, 8], F32, kind="ExternalInput").ap()
        idd = nc.dram_tensor("identc", [128, 128], F32, kind="ExternalInput").ap()
        oned = nc.dram_tensor("onesc", [128, 128], F32, kind="ExternalInput").ap()
        fcd = nc.dram_tensor("fconstc", [128, 8], F32, kind="ExternalInput").ap()
        xo = nc.dram_tensor("xo", [S, DM], F32, kind="ExternalOutput").ap()

        u = alloc_common(nc, stack)
        sbl = lambda name, shape, dt: stack.enter_context(
            nc.sbuf_tensor("sb_" + name, shape, dt))
        u["lnt"] = sbl("lnt", [128, NT * DM], BF16)
        u["lnc"] = sbl("lnc", [128, NC * S], BF16)
        u["b1x"] = sbl("b1x", [128, 256], F32)
        u["b2r"] = sbl("b2r", [8, 1024], BF16)
        alloc_moe(nc, stack, u, E=8, maxf=1024)
        u["h"] = u["lnt"]

        b.dma(u["ident"][:, :], idd, sem="dc")
        b.dma(u["ones"][:, :], oned, sem="dc")
        b.dma(u["fc"][:, :], fcd, sem="dc")
        b.dma(u["b1x"][:, :], b1d, sem="dc")
        t_c = b.tok("dc")
        t_x = b.dma(sb3(u["x"][:, :], NT), xd.rearrange("(t p) d -> p t d", p=128),
                    eng="sync", sem="ds")
        b.wait_only("vector", [t_x, t_c])
        b.wait_only("scalar", [t_x, t_c])
        b.wait_only("tensor", [t_c])
        norm_toks, lnc_tok, tp_tok, lnc_tt = emit_ln_lnc(b, u)
        t_all = emit_moe(
            b, u, u["lnc"], u["x"], E=8, F=4096, fchunk=1024,
            w1_src=lambda e, q: cmaj(w1d[e], 0, DM, q * 1024, (q + 1) * 1024),
            w2_src=lambda e, q: cmaj(w2d[e], q * 1024, (q + 1) * 1024, 0, DM),
            router_src=cmaj(rtd, 0, DM, 0, 8),
            b1_col=lambda e, q, fs: u["b1x"][:, e * 32 + q * 8 + fs:
                                             e * 32 + q * 8 + fs + 1],
            b2_src=b2d.rearrange("o (e h) -> (o e) h", e=8),
            lnc_tok=lnc_tok, dst_waits=lambda ta: [norm_toks[ta]],
            scratch_free=[tp_tok], wbuf_free=[], lnc_tt=lnc_tt)
        t_st = b.dma(xo.rearrange("(t p) d -> p t d", p=128), sb3(u["x"][:, :], NT),
                     waits=[t_all], eng="sync", sem="ds")
        b.wait_only("sync", [t_st])
    return build_nc(bf)


def p1_maps(P, c):
    return {"x": None, "w1t": P["l0_w1t"], "w2t": P["l0_w2t"],
            "b1x": P["l0_b1"], "b2r": P["l0_b2"], "routerT": P["l0_routerT"],
            "identc": P["ident"], "onesc": P["ones"], "fconstc": P["fconst"]}


def build_p3(P):
    def bf(nc, stack, b):
        xd = nc.dram_tensor("x", [S, DM], F32, kind="ExternalInput").ap()
        cad = nc.dram_tensor("cattn", [DM, 3 * DM], F32, kind="ExternalInput").ap()
        cpd = nc.dram_tensor("cproj", [DM, DM], F32, kind="ExternalInput").ap()
        l1d = nc.dram_tensor("ln1b", [128, DM], F32, kind="ExternalInput").ap()
        l2d = nc.dram_tensor("ln2b", [128, DM], F32, kind="ExternalInput").ap()
        rtd = nc.dram_tensor("router", [DM, 8], F32, kind="ExternalInput").ap()
        w1d = nc.dram_tensor("w1", [DM, 16384], F32, kind="ExternalInput").ap()
        w2d = nc.dram_tensor("w2", [16384, DM], F32, kind="ExternalInput").ap()
        idd = nc.dram_tensor("identc", [128, 128], F32, kind="ExternalInput").ap()
        oned = nc.dram_tensor("onesc", [128, 128], F32, kind="ExternalInput").ap()
        cmd = nc.dram_tensor("cmaskc", [128, 2048], F32, kind="ExternalInput").ap()
        fcd = nc.dram_tensor("fconstc", [128, 8], F32, kind="ExternalInput").ap()
        xo = nc.dram_tensor("xo", [S, DM], F32, kind="ExternalOutput").ap()

        u = alloc_common(nc, stack)
        sbl = lambda name, shape, dt: stack.enter_context(
            nc.sbuf_tensor("sb_" + name, shape, dt))
        u["lnt"] = sbl("lnt", [128, NT * DM], BF16)
        u["lnc"] = sbl("lnc", [128, NC * S], BF16)
        u["lnw"] = sbl("lnw", [128, 2 * DM], F32)
        u["lnw1"] = u["lnw"][:, 0:DM]
        u["lnw2"] = u["lnw"][:, DM:2 * DM]
        A3 = sbl("A3", [128, 32768], BF16)
        u["qT"] = A3[:, 0:2048]
        u["kT"] = A3[:, 2048:4096]
        u["vON"] = A3[:, 4096:7168]
        u["probsT"] = A3[:, 7168:15360]
        u["wqkv"] = A3[:, 15360:21504]
        u["wo"] = A3[:, 21504:25600]
        u["invr"] = A3[:, 25600:26112]
        u["tmp64"] = A3[:, 26112:26624]
        u["cmask"] = A3[:, 26624:28672]
        u["aoT"] = u["lnt"]
        u["w1b"] = A3[:, 0:16384]
        u["w2b"] = A3[:, 16384:32768]
        u["xf"] = A3.bitcast(F32)
        u["h"] = u["lnt"]
        alloc_moe(nc, stack, u, E=8, maxf=1024, weights=False)

        b.dma(u["ident"][:, :], idd, sem="dc")
        b.dma(u["ones"][:, :], oned, sem="dc")
        b.dma(u["fc"][:, :], fcd, sem="dc")
        b.dma(u["cmask"], cmd, sem="dc")
        b.dma(u["lnw"][:, 0:DM], l1d, sem="dc")
        b.dma(u["lnw"][:, DM:2 * DM], l2d, sem="dc")
        t_c = b.tok("dc")
        t_x = b.dma(sb3(u["x"][:, :], NT), xd.rearrange("(t p) d -> p t d", p=128),
                    eng="sync", sem="ds")
        b.wait_only("vector", [t_x, t_c])
        b.wait_only("scalar", [t_x, t_c])
        b.wait_only("tensor", [t_c])
        emit_l1ffn(b, u, {"cattn": cad, "cproj": cpd, "router": rtd,
                          "w1": w1d, "w2": w2d}, xd, xo)
    return build_nc(bf)


def emit_l1ffn(b, u, dr, x_src_hbm, x_dst_hbm, phase_free=(), spilled=False,
               xtiles=None, sq_early=None, const_tok=None):
    sqe = sq_early if sq_early is not None else u["sq"]
    nt0 = emit_ln(b, u["x"][:, :], sqe, u["st"],
                  lambda ti: u["x"][:, ti * DM:(ti + 1) * DM], u["fc"][:, 0:1],
                  norm_free=lambda ti: list(phase_free) +
                  ([("dsp%d" % (ti // 4), 16)] if spilled else []),
                  x_ready=(lambda ti: [xtiles[ti]]) if xtiles else None)
    nt1 = emit_ln(b, u["x"][:, :], sqe, u["st"],
                  lambda ti: u["lnt"][:, ti * DM:(ti + 1) * DM], u["fc"][:, 0:1],
                  wrow=u["lnw1"],
                  x_ready=lambda ti: [nt0[ti]] + ([nt0[-1]] if ti == 0 else []),
                  wrow_tok=[const_tok] if const_tok else ())
    lnc_tok, tp_tok, lnc_tt = emit_lnt_to_lnc(b, u["lnt"], u["lnc"], u["pt"],
                                              u["ident"][:, :], nt1,
                                              extra_wait=list(phase_free))
    w_srcs = lambda g: (cmaj(dr["cattn"], 0, DM, g * 128, (g + 1) * 128),
                        cmaj(dr["cattn"], 0, DM, DM + g * 128, DM + (g + 1) * 128),
                        cmaj(dr["cattn"], 0, DM, 2 * DM + g * 128,
                             2 * DM + (g + 1) * 128))
    wo_src = lambda dh: cmaj(dr["cproj"], 0, DM, dh * 512, (dh + 1) * 512)
    t_attn, t_heads = emit_attention(b, u, u["lnc"], u["x"], w_srcs, wo_src,
                                     lnc_tok, [tp_tok], lambda tt: [nt1[tt]],
                                     list(phase_free), list(phase_free),
                                     lnc_tt=lnc_tt, const_tok=const_tok)
    nt2 = emit_ln(b, u["x"][:, :], u["sq"], u["st"],
                  lambda ti: u["lnt"][:, ti * DM:(ti + 1) * DM], u["fc"][:, 0:1],
                  wrow=u["lnw2"], x_ready=lambda ti: [t_attn],
                  norm_free=lambda ti: [t_attn],
                  wrow_tok=[const_tok] if const_tok else ())
    lnc_tok2, tp_tok2, lnc_tt2 = emit_lnt_to_lnc(b, u["lnt"], u["lnc"],
                                                 u["pt"], u["ident"][:, :],
                                                 nt2, extra_wait=[t_attn])
    t_moe = emit_moe(
        b, u, u["lnc"], u["x"], E=8, F=2048, fchunk=1024,
        w1_src=lambda e, q: cmaj(dr["w1"], 0, DM, e * 2048 + q * 1024,
                                 e * 2048 + (q + 1) * 1024),
        w2_src=lambda e, q: cmaj(dr["w2"], e * 2048 + q * 1024,
                                 e * 2048 + (q + 1) * 1024, 0, DM),
        router_src=cmaj(dr["router"], 0, DM, 0, 8),
        b1_col=None, b2_src=None,
        lnc_tok=lnc_tok2, dst_waits=lambda ta: [nt2[ta]],
        scratch_free=[tp_tok2, t_attn], wbuf_free=[t_heads], lnc_tt=lnc_tt2,
        wbuf2_free=[t_attn, nt2[-1]])
    src3 = x_src_hbm.rearrange("(t p) d -> p t d", p=128)
    dst3 = x_dst_hbm.rearrange("(t p) d -> p t d", p=128)
    xb3 = sb3(u["x"][:, :], NT)
    for qq in range(4):
        t_acc = b.emit("gpsimd", lambda e, qq=qq: e.dma_start(
            xb3[:, qq * 4:(qq + 1) * 4, :], src3[:, qq * 4:(qq + 1) * 4, :],
            accum_op=ALU.add),
            waits=[t_moe] if qq == 0 else (), inc="dac%d" % qq, inc_amt=16)
        b.dma(dst3[:, qq * 4:(qq + 1) * 4, :], xb3[:, qq * 4:(qq + 1) * 4, :],
              waits=[t_acc], eng="sync", sem="ds")
    t_st = b.tok("ds")
    b.wait_only("sync", [t_st])
    return t_st


def p3_maps(P, c):
    return {"x": None, "cattn": P["l1_cattn"], "cproj": P["l1_cproj"],
            "ln1b": P["l1_ln1_b"], "ln2b": P["l1_ln2_b"],
            "router": P["l1_router"], "w1": P["l1_w1"], "w2": P["l1_w2"],
            "identc": P["ident"], "onesc": P["ones"], "cmaskc": P["cmask"],
            "fconstc": P["fconst"]}


def build_fused(P):
    def bf(nc, stack, b):
        dt_in = lambda n, shp: nc.dram_tensor(n, shp, F32, kind="ExternalInput").ap()
        xd = dt_in("x", [S, DM])
        qd, kd, vd, od = (dt_in("qT", [DM, DM]), dt_in("kT", [DM, DM]),
                          dt_in("vT", [DM, DM]), dt_in("oT", [DM, DM]))
        w1d0 = dt_in("w1t", [8, DM, 4096])
        w2d0 = dt_in("w2t", [8, 4096, DM])
        b1d = dt_in("b1x", [128, 256])
        b2d = dt_in("b2r", [1, 8192])
        rtd0 = dt_in("routerT", [DM, 8])
        dd, ad_, bd_, gd = (dt_in("delta", [128, 128]), dt_in("alphap", [128, 128]),
                            dt_in("betap", [128, 128]), dt_in("gammap", [128, 128]))
        odb = dt_in("omegab", [128, DM])
        td = dt_in("tvalsc", [128, 2048])
        bdd = dt_in("bdiagc", [128, 2048])
        wfd = nc.dram_tensor("dftW", [NFT, 2, 16, 128, 128], BF16,
                             kind="ExternalInput").ap()
        cvd = nc.dram_tensor("dftC", [NFT, 2, 16, 128, 128], BF16,
                             kind="ExternalInput").ap()
        cad = dt_in("cattn", [DM, 3 * DM])
        cpd = dt_in("cproj", [DM, DM])
        l1d = dt_in("ln1b", [128, DM])
        l2d = dt_in("ln2b", [128, DM])
        rtd1 = dt_in("router", [DM, 8])
        w1d1 = dt_in("w1", [DM, 16384])
        w2d1 = dt_in("w2", [16384, DM])
        idd = dt_in("identc", [128, 128])
        oned = dt_in("onesc", [128, 128])
        cmd = dt_in("cmaskc", [128, 2048])
        fcd = dt_in("fconstc", [128, 8])
        xo = nc.dram_tensor("xo", [S, DM], F32, kind="ExternalOutput").ap()
        xsp = nc.dram_tensor("xspill", [S, DM], F32, kind="Internal").ap()

        u = alloc_common(nc, stack)
        sbl = lambda name, shape, dt: stack.enter_context(
            nc.sbuf_tensor("sb_" + name, shape, dt))
        AR = sbl("AR", [128, 68608], BF16)
        ARF = AR.bitcast(F32)
        u["lnt"] = AR[:, 0:16384]
        u["lnc"] = AR[:, 16384:32768]
        u["lnw1"] = ARF[:, 32768:33792]
        u["lnw2"] = ARF[:, 31744:32768]
        u["b1x"] = sbl("b1x", [128, 256], F32)
        u["b2r"] = sbl("b2r", [8, 1024], BF16)
        alloc_moe(nc, stack, u, weights=False)
        at = {"qT": AR[:, 32768:34816], "kT": AR[:, 34816:36864],
              "vON": AR[:, 36864:39936], "probsT": AR[:, 39936:48128],
              "wqkv": AR[:, 48128:54272], "wo": AR[:, 54272:58368],
              "invr": AR[:, 58368:58880], "tmp64": AR[:, 58880:59392],
              "cmask": AR[:, 59392:61440], "aoT": u["lnt"]}
        f0 = {"w1b": AR[:, 32768:49152], "w2b": AR[:, 49152:65536],
              "h": u["lnt"]}
        cv = {"kt": AR[:, 16384:32768], "ksb": AR[:, 32768:49152],
              "zr": AR[:, 32768:41472], "zi": AR[:, 41472:50176],
              "wstr": AR[:, 50176:58368],
              "e2f": ARF[:, 25088:27136],
              "e2m": AR[:, 54272:58368],
              "tvals": ARF[:, 29184:31232],
              "pw": ARF[:, 29184:32256],
              "prm": ARF[:, 32256:33280],
              "bdiag": AR[:, 66560:68608],
              "omgb": ARF[:, 33280:34304]}
        u.update(at)

        b.dma(u["ident"][:, :], idd, sem="dc")
        b.dma(u["ones"][:, :], oned, sem="dc")
        b.dma(u["fc"][:, :], fcd, sem="dc")
        b.dma(u["cmask"], cmd, sem="dc")
        b.dma(u["b1x"][:, :], b1d, sem="dc")
        t_c = b.tok("dc")
        t_x = b.dma(sb3(u["x"][:, :], NT), xd.rearrange("(t p) d -> p t d", p=128),
                    eng="sync", sem="ds")
        b.wait_only("vector", [t_x, t_c])
        b.wait_only("scalar", [t_x, t_c])
        b.wait_only("tensor", [t_c])

        emit_attn_block(b, u, {"q": qd, "k": kd, "v": vd, "o": od})
        b.join(("tensor", "scalar", "vector", "gpsimd", "sync"))
        u1 = dict(u); u1.update(f0)
        norm_toks, lnc_tok, tp_tok, lnc_tt = emit_ln_lnc(b, u1)
        emit_moe(
            b, u1, u1["lnc"], u1["x"], E=8, F=4096, fchunk=1024,
            w1_src=lambda e, q: cmaj(w1d0[e], 0, DM, q * 1024, (q + 1) * 1024),
            w2_src=lambda e, q: cmaj(w2d0[e], q * 1024, (q + 1) * 1024, 0, DM),
            router_src=cmaj(rtd0, 0, DM, 0, 8),
            b1_col=lambda e, q, fs: u1["b1x"][:, e * 32 + q * 8 + fs:
                                              e * 32 + q * 8 + fs + 1],
            b2_src=b2d.rearrange("o (e h) -> (o e) h", e=8),
            lnc_tok=lnc_tok, dst_waits=lambda ta: [norm_toks[ta]],
            scratch_free=[tp_tok], wbuf_free=[], lnc_tt=lnc_tt)
        b.join(("tensor", "scalar", "vector", "gpsimd", "sync"))
        u2 = dict(u); u2.update(cv)
        b.dma(u2["tvals"], td, sem="dc")
        b.dma(u2["bdiag"], bdd, sem="dc")
        t_c2 = b.tok("dc")
        b.wait_only("scalar", [t_c2])
        b.wait_only("tensor", [t_c2])
        nt_c = emit_ln(b, u2["x"][:, :], u2["sq"], u2["st"],
                       lambda ti: u2["lnt"][:, ti * DM:(ti + 1) * DM],
                       u2["fc"][:, 0:1])
        t_cv, xtiles = emit_conv(b, u2, {"delta": dd, "alpha": ad_,
                                         "beta": bd_, "gamma": gd,
                                         "dftW": wfd, "dftC": cvd,
                                         "omegab": odb},
                                 nt_c, scratch_free=[])
        b.join(("tensor", "gpsimd"))
        xsp3 = xsp.rearrange("(t p) d -> p t d", p=128)
        for qq in range(4):
            b.dma(xsp3[:, qq * 4:(qq + 1) * 4, :],
                  sb3(u["x"][:, :], NT)[:, qq * 4:(qq + 1) * 4, :],
                  eng="sync", sem="dsp%d" % qq,
                  waits=[xtiles[qq * 4 + 3]])
        t_sp = None
        u3 = dict(u); u3.update(at)
        u3["w1b"] = AR[:, 32768:49152]
        u3["w2b"] = AR[:, 49152:65536]
        u3["h"] = u3["lnt"]
        u3["xf"] = AR[:, 32768:65536].bitcast(F32)
        b.dma(u3["cmask"], cmd, sem="dc")
        b.dma(u3["lnw1"], l1d, sem="dc")
        b.dma(u3["lnw2"], l2d, sem="dc")
        t_c3 = b.tok("dc")
        emit_l1ffn(b, u3, {"cattn": cad, "cproj": cpd, "router": rtd1,
                           "w1": w1d1, "w2": w2d1}, xsp, xo,
                   phase_free=[], spilled=True, xtiles=xtiles,
                   sq_early=ARF[:, 30720:31744], const_tok=t_c3)
    return build_nc(bf)


def fused_maps(P, c):
    return {"x": None, "qT": P["l0_qT"], "kT": P["l0_kT"], "vT": P["l0_vT"],
            "oT": P["l0_oT"], "w1t": P["l0_w1t"], "w2t": P["l0_w2t"],
            "b1x": P["l0_b1"], "b2r": P["l0_b2"], "routerT": P["l0_routerT"],
            "delta": P["l1_delta"], "alphap": P["l1_alpha"],
            "betap": P["l1_beta"], "gammap": P["l1_gamma"],
            "omegab": P["l1_omega_b"], "tvalsc": P["tvals"],
            "bdiagc": P["bdiag"], "dftW": P["dftW"], "dftC": P["dftC"],
            "cattn": P["l1_cattn"], "cproj": P["l1_cproj"],
            "ln1b": P["l1_ln1_b"], "ln2b": P["l1_ln2_b"],
            "router": P["l1_router"], "w1": P["l1_w1"], "w2": P["l1_w2"],
            "identc": P["ident"], "onesc": P["ones"], "cmaskc": P["cmask"],
            "fconstc": P["fconst"]}


def p0_maps(P, c):
    return {"x": None, "qT": P["l0_qT"], "kT": P["l0_kT"],
            "vT": P["l0_vT"], "oT": P["l0_oT"], "identc": P["ident"],
            "onesc": P["ones"], "cmaskc": P["cmask"], "fconstc": P["fconst"]}


def _run_phase(build_fn, maps_fn, P, x):
    nc = build_fn(P)
    maps = []
    for c in range(NCORES):
        m = maps_fn(P, c)
        m["x"] = x[c]
        maps.append(m)
    res = run_bass_kernel_spmd(nc, maps, core_ids=list(range(NCORES)))
    return np.stack([r["xo"] for r in res.results])


FUSED = True


def kernel(**inputs):
    P = prep_inputs(inputs)
    x = P["x"]
    if FUSED:
        nc = build_fused(P)
        maps = []
        for c in range(NCORES):
            m = fused_maps(P, c)
            m["x"] = x[c]
            maps.append(m)
        res = run_bass_kernel_spmd(nc, maps, core_ids=list(range(NCORES)))
        return np.ascontiguousarray(np.stack([r["xo"] for r in res.results]))
    x = _run_phase(build_p0, p0_maps, P, x)
    x = _run_phase(build_p1, p1_maps, P, x)
    x = _run_phase(build_p2, p2_maps, P, x)
    x = _run_phase(build_p3, p3_maps, P, x)
    return np.ascontiguousarray(x)
